# version 24
# baseline (speedup 1.0000x reference)
import contextlib
import numpy as np
import ml_dtypes
import concourse.bass as bass
import concourse.mybir as mybir
from concourse.bass_utils import run_bass_kernel_spmd
from concourse.alu_op_type import AluOpType
import concourse.dve_ops as dve_ops
from concourse.dve_op import DveOp
from concourse.dve_spec import Spec, Src0, Src1, C0, C1, C2, Zero, select, maxx, relu
from concourse.dve_spec import lower as _spec_lower

F32 = mybir.dt.float32
BF16 = mybir.dt.bfloat16
FP8 = mybir.dt.float8e4
AF = mybir.ActivationFunctionType
AOP = AluOpType


def _fq(x, man, min_exp, max_val):
    m, e = np.frexp(x)
    eff = np.maximum(e - 1, min_exp)
    scale = np.ldexp(np.ones_like(x), eff - man)
    with np.errstate(invalid="ignore"):
        q = np.round(np.divide(x, scale, where=scale != 0, out=np.zeros_like(x))) * scale
    q = np.where(np.abs(q) > max_val, np.sign(x) * np.inf, q)
    return np.where(x == 0, x, q)


def _rne_sig(x, nbits):
    m, e = np.frexp(x)
    scale = np.ldexp(np.ones_like(x), e - nbits)
    with np.errstate(invalid="ignore"):
        q = np.round(np.divide(x, scale, where=scale != 0, out=np.zeros_like(x))) * scale
    return np.where(x == 0, x, q)


def _sbits(s0):
    c = np.asarray(s0).ravel()[0] if not np.isscalar(s0) else s0
    return 24 - int(round(np.log2(float(c) - 1.0)))


def _ref_qrne(in0, in1, s0, s1, imm2):
    return _rne_sig(in0, _sbits(s0)).astype(in0.dtype)


def _ref_qsubn(in0, in1, s0, s1, imm2):
    A = np.asarray(s0).ravel()[0] if not np.isscalar(s0) else s0
    G = float(A) / (1.5 * 2 ** 23)
    lo = np.round(in1 / G) * G
    q = np.where(np.abs(in1) < imm2, lo, in0)
    q = np.where(np.abs(q) > 240.0, np.sign(in1) * np.inf, q)
    return q.astype(in0.dtype)


def _ref_qhq(in0, in1, s0, s1, imm2):
    z = np.maximum(in0 + s0, 0)
    return _fq(z, 3, -6, 240.0).astype(in0.dtype)


def _ref_qout(in0, in1, s0, s1, imm2):
    s0q = _fq(np.asarray(s0, dtype=in0.dtype), 10, -14, 65504.0)
    return (_fq(in0 + s0q, 10, -14, 65504.0) + in1).astype(in0.dtype)


def _mkop(name, spec, ref, row_donor=None):
    for o in dve_ops.OPS:
        if o.name == name:
            return o
    sha3 = _spec_lower(spec, ver="v3", perf_en=False, name=name).sha("v3")
    op = DveOp(name, spec, subdim=False, uops_sha={"v3": sha3}, reference=ref)
    dve_ops.OPS.append(op)
    dve_ops.CUSTOM_DVE_SPECS[name] = spec
    dve_ops.CUSTOM_DVE_REFS[name] = ref
    if row_donor is None:
        row = max(dve_ops._SUB_OPCODE_FOR_NAME.values()) + 1
        assert row < 0x20, row
    else:
        row = dve_ops._SUB_OPCODE_FOR_NAME[row_donor]
    dve_ops._SUB_OPCODE_FOR_NAME[name] = row
    return op


_pA = Src0 * C0
Q_RNE = _mkop("Q_RNE", Spec(body=_pA + (Src0 - _pA)), _ref_qrne)
_zB = relu(Src0 + C0)
_pB = _zB * C1
Q_HQ = _mkop("Q_HQ", Spec(body=_pB + (_zB - _pB)), _ref_qhq)
_azC = maxx(Src1, Zero - Src1)
_loC = (Src1 + C0) - C0
Q_SUBN = _mkop("Q_SUBN", Spec(body=select(_azC < C2, _loC, Src0)), _ref_qsubn,
               row_donor="DVEHASH32L_SEED")
_sD = Src0 + C0
_pD = _sD * C1
Q_OUT = _mkop("Q_OUT", Spec(body=(_pD + (_sD - _pD)) + Src1), _ref_qout,
              row_donor="DVEHASH32L_BODY")


def _ref_qapl(in0, in1, s0, s1, imm2):
    return ((in0 - in1) * s0 + s1).astype(in0.dtype)


Q_APL = _mkop("Q_APL", Spec(body=(Src0 - Src1) * C0 + C1), _ref_qapl,
              row_donor="DVEHASH32L_DRAIN")

C_E4 = float(2.0 ** 20 + 1)
C_F16 = float(2.0 ** 13 + 1)
A_E4 = float(1.5 * 2 ** 23 * 2.0 ** -9)
THR_E4 = float(2.0 ** -6)

NJ, NSC, SC, NH, NSH = 8, 4, 512, 32, 2
INV_D = float(1.0 / 1024.0)
LN_EPS = 1e-5


class P:
    COMPUTE = ("pe", "dve", "act", "gp")
    DMAS = ("sdc", "sdx", "sd0", "sd1", "gd0", "gd1", "gdo0", "gdo1", "gdo2", "gdo3",
            "gax", "gaw", "gw0", "gw1",
            "gx0", "gx1", "gx2", "gx3", "gx4", "gx5", "gx6", "gx7")
    ENGS = COMPUTE + DMAS

    def __init__(self):
        self.q = {"pe": [], "dve": [], "act": [], "sd": [], "gd": []}
        self.n = {k: 0 for k in self.ENGS}

    def emit(self, eng, fn, waits=(), sem=None):
        sem = sem or eng
        inc = 16 if sem in self.DMAS else 1
        wl = {}
        if sem in ("dve", "act", "gp") and self.n[sem] > 0:
            wl[sem] = self.n[sem]
        for t in waits:
            if t is None:
                continue
            wk, wv = t
            if wv <= 0 or (wk == sem and wk in self.COMPUTE):
                continue
            if wk not in wl or wv > wl[wk]:
                wl[wk] = wv
        wl = tuple(wl.items())
        self.n[sem] += inc
        tgt = self.n[sem]

        def go(e, sems, _fn=fn, _wl=wl, _sem=sem, _inc=inc):
            for wk, wv in _wl:
                e.wait_ge(sems[wk], wv)
            _fn(e).then_inc(sems[_sem], _inc)

        self.q[eng].append(go)
        return (sem, tgt)


def build(nc):
    dt = nc.dram_tensor
    xT = dt("xT", [1024, 2048], F32, kind="ExternalInput").ap()
    wvB = dt("wvB", [128, 8192], F32, kind="ExternalInput").ap()
    wqB = dt("wqB", [128, 8192], F32, kind="ExternalInput").ap()
    wgB = dt("wgB", [128, 8192], F32, kind="ExternalInput").ap()
    woB = dt("woB", [128, 8192], F32, kind="ExternalInput").ap()
    w1B = dt("w1B", [4, 128, 8192], F32, kind="ExternalInput").ap()
    w2B = dt("w2B", [4, 128, 8192], F32, kind="ExternalInput").ap()
    b1c = dt("b1c", [128, 32], F32, kind="ExternalInput").ap()
    b2c = dt("b2c", [128, 8], F32, kind="ExternalInput").ap()
    gamc = dt("gamc", [128, 8], F32, kind="ExternalInput").ap()
    betc = dt("betc", [128, 8], F32, kind="ExternalInput").ap()
    lamc = dt("lamc", [128, 8], F32, kind="ExternalInput").ap()
    onesb = dt("onesb", [128, 128], BF16, kind="ExternalInput").ap()
    epsc = dt("epsc", [128, 1], F32, kind="ExternalInput").ap()
    outT = dt("outT", [1024, 2048], F32, kind="ExternalOutput").ap()

    st = contextlib.ExitStack()
    mk = st.enter_context
    XT = mk(nc.sbuf_tensor("XT", [128, 16384], BF16))
    XNT = mk(nc.sbuf_tensor("XNT", [128, 16384], BF16))
    WST = [mk(nc.sbuf_tensor(f"WST{i}", [128, 2048], F32)) for i in range(2)]
    WBF = [mk(nc.sbuf_tensor(f"WBF{i}", [128, 8192], BF16)) for i in range(2)]
    RBC = mk(nc.sbuf_tensor("RBC", [128, 2048], BF16))
    RMBC = mk(nc.sbuf_tensor("RMBC", [128, 2048], BF16))
    B1C = mk(nc.sbuf_tensor("B1C", [128, 32], F32))
    B2C = mk(nc.sbuf_tensor("B2C", [128, 8], F32))
    B2Q = mk(nc.sbuf_tensor("B2Q", [128, 8], F32))
    GAM = mk(nc.sbuf_tensor("GAM", [128, 8], F32))
    BET = mk(nc.sbuf_tensor("BET", [128, 8], F32))
    LAM = mk(nc.sbuf_tensor("LAM", [128, 8], F32))
    ONES = mk(nc.sbuf_tensor("ONES", [128, 128], BF16))
    EPSC = mk(nc.sbuf_tensor("EPSC", [128, 1], F32))
    TOPA = 147424
    ST = nc.alloc_sbuf_tensor_at("ST", [128, 16384], BF16, offset=TOPA)
    HQ = nc.alloc_sbuf_tensor_at("HQ", [128, 32768], FP8, offset=TOPA)
    XST = [nc.alloc_sbuf_tensor_at(f"XST{i}", [128, 2048], F32, offset=TOPA + 32768 + 8192 * i)
           for i in range(2)] + [
          nc.alloc_sbuf_tensor_at(f"XST{i + 2}", [128, 2048], F32, offset=TOPA + 8192 * i)
           for i in range(4)]
    RA = nc.alloc_sbuf_tensor_at("RA", [128, 2048], F32, offset=TOPA + 49152)
    HIQE = nc.alloc_sbuf_tensor_at("HIQE", [128, 2048], BF16, offset=TOPA - 4096)
    XSTP = [nc.alloc_sbuf_tensor_at(f"XSTP{i}", [128, 4096], F32, offset=o)
            for i, o in enumerate((TOPA, TOPA + 16384, TOPA + 32768))]
    RB = nc.alloc_sbuf_tensor_at("RB", [128, 2048], F32, offset=TOPA + 57344)
    W2Q = nc.alloc_sbuf_tensor_at("W2Q", [128, 32768], FP8, offset=TOPA + 32768)
    XSQ = [nc.alloc_sbuf_tensor_at(f"XSQ{i}", [128, 2048], BF16, offset=TOPA + 65536 + 4096 * i)
           for i in range(2)]
    HIQ = nc.alloc_sbuf_tensor_at("HIQ", [128, 2048], BF16, offset=TOPA + 65536)
    VST = [nc.alloc_sbuf_tensor_at(f"VST{i}", [128, 2048], BF16, offset=TOPA + 73728 + 4096 * i)
           for i in range(2)]
    RC = nc.alloc_sbuf_tensor_at("RC", [128, 2048], F32, offset=TOPA + 65536)
    OUTS = nc.alloc_sbuf_tensor_at("OUTS", [128, 2048], F32, offset=TOPA + 73728)
    PS = [mk(nc.psum_tensor(f"PS{i}", [128, 512], F32)) for i in range(8)]
    PSA = PS[:4]
    PSB = PS[4:]
    sems = {k: mk(nc.semaphore(f"sem_{k}")) for k in P.ENGS}
    blk = mk(nc.Block())

    p = P()
    E = p.emit

    t_consts = None
    for dst, src in ((B1C, b1c), (B2C, b2c), (GAM, gamc), (BET, betc),
                     (LAM, lamc), (ONES, onesb), (EPSC, epsc)):
        t_consts = E("sd", lambda e, d=dst, s=src: e.dma_start(d[:], s), sem="sdc")
    t_b2q = E("dve", lambda e: e._custom_dve(Q_RNE, out=B2Q[:], in0=B2C[:], s0=C_F16),
              waits=[t_consts])

    t_xdma = [None] * NJ
    _act0 = p.n["act"]
    _dve0 = p.n["dve"]
    _sq_pos = {j: _dve0 + 1 + j for j in range(NJ)}
    for j in range(NJ):
        w = [("act", _act0 + 1)] if j >= 6 else []
        qeng = ["gd", "sd"][j % 2]
        pt, hh = [0, 1, 2, 0][j // 2], j % 2
        t_xdma[j] = E(qeng, lambda e, pt=pt, hh=hh, j=j: e.dma_start(
            XSTP[pt][:, hh * 2048:(hh + 1) * 2048],
            xT[j * 128:(j + 1) * 128, :]), waits=w, sem=f"gx{j}")

    wst_free = [None, None]
    t_wbf = {}
    wu = [0]

    def stream_w(name, slot, dram_ap, free_after, cast_eng="act",
                 qpat=("sd", "sd", "sd", "sd"), units=(0, 1, 2, 3)):
        toks = t_wbf.get(name, [None] * 4)
        for u in units:
            z = wu[0] % 2
            wu[0] += 1
            if qpat[u] == "act":
                t_d = E("act", lambda e, z=z, u=u, d=dram_ap: e.dma_start(
                    WST[z][:], d[:, u * 2048:(u + 1) * 2048]), waits=[wst_free[z]],
                        sem="gaw")
            else:
                t_d = E("sd", lambda e, z=z, u=u, d=dram_ap: e.dma_start(
                    WST[z][:], d[:, u * 2048:(u + 1) * 2048]), waits=[wst_free[z]],
                        sem=f"sd{z}")
            if cast_eng == "act":
                t_c = E("act", lambda e, z=z, u=u, slot=slot: e.activation(
                    WBF[slot][:, u * 2048:(u + 1) * 2048], WST[z][:], AF.Copy),
                    waits=[t_d, free_after])
            elif cast_eng == "gp":
                t_c = E("gd", lambda e, z=z, u=u, slot=slot: e.tensor_copy(
                    WBF[slot][:, u * 2048:(u + 1) * 2048], WST[z][:]),
                    waits=[t_d, free_after], sem="gp")
            else:
                t_c = E("dve", lambda e, z=z, u=u, slot=slot: e.tensor_copy(
                    WBF[slot][:, u * 2048:(u + 1) * 2048], WST[z][:]),
                    waits=[t_d, free_after])
            wst_free[z] = t_c
            toks[u] = t_c
        t_wbf[name] = toks

    def stream_w2_unit(qq, u, first_waits):
        z = wu[0] % 2
        wu[0] += 1
        fw = list(first_waits)
        t_d = E("sd", lambda e, z=z, u=u, qq=qq: e.dma_start(
            WST[z][:], w2B[qq, :, u * 2048:(u + 1) * 2048]), waits=[wst_free[z]],
                sem=f"sd{z}")
        t_hi = E("dve", lambda e, z=z: e._custom_dve(
            Q_RNE, out=HIQE[:], in0=WST[z][:], s0=C_E4), waits=[t_d] + fw)
        t_q = E("dve", lambda e, z=z, qq=qq, u=u: e._custom_dve(
            Q_SUBN, out=W2Q[:, qq * 8192 + u * 2048: qq * 8192 + (u + 1) * 2048],
            in0=HIQE[:], in1=WST[z][:], s0=A_E4, imm2=THR_E4), waits=fw + [t_d])
        wst_free[z] = t_q
        return t_q

    ps_free = [None] * 8
    class _PFa:
        def __getitem__(self, i): return ps_free[i]
        def __setitem__(self, i, v): ps_free[i] = v
    class _PFb:
        def __getitem__(self, i): return ps_free[4 + i]
        def __setitem__(self, i, v): ps_free[4 + i] = v
    psa_free = _PFa()
    psb_free = _PFb()

    def ln_stats_tile(j, src_tok, sq_tok, src_sb):
        seen = set()
        for sc in range(NSC):
            w_all = [src_tok, t_consts, psa_free[sc]]
            w = [t for t in w_all if t is not None and t not in seen]
            seen.update(w)
            E("pe", lambda e, j=j, sc=sc, src_sb=src_sb: e.matmul(
                PSA[sc][:, :], ONES[:],
                src_sb[:, j * 2048 + sc * 512: j * 2048 + (sc + 1) * 512],
                start=(j == 0), stop=(j == NJ - 1)), waits=w)
        last_pe = None
        for sc in range(NSC):
            w_all = [sq_tok, t_consts, psb_free[sc]]
            w = [t for t in w_all if t is not None and t not in seen]
            seen.update(w)
            last_pe = E("pe", lambda e, j=j, sc=sc: e.matmul(
                PSB[sc][:, :], ONES[:], XSQ[j % 2][:, sc * 512:(sc + 1) * 512],
                start=(j == 0), stop=(j == NJ - 1)), waits=w)
        return last_pe

    def ln_stats_evict(last_pe, extra_a, extra_b):
        ev = []
        for sc in range(NSC):
            ta = E("act", lambda e, sc=sc: e.activation(
                RA[:, sc * 512:(sc + 1) * 512], PSA[sc][:, :], AF.Copy),
                waits=[last_pe] + extra_a)
            tb = E("act", lambda e, sc=sc: e.activation(
                RB[:, sc * 512:(sc + 1) * 512], PSB[sc][:, :], AF.Copy),
                waits=[last_pe] + extra_b)
            psa_free[sc] = ta
            psb_free[sc] = tb
            ev.extend([ta, tb])
        return ev

    def ln_rows(ev, rbc_free, rmbc_free):
        t1 = E("dve", lambda e: e.tensor_scalar_mul(RA[:], RA[:], INV_D), waits=ev)
        t3 = E("dve", lambda e: e.tensor_mul(RC[:], RA[:], RA[:]), waits=ev)
        t4 = E("dve", lambda e: e.scalar_tensor_tensor(RB[:], RB[:], INV_D, RC[:],
                                                       AOP.mult, AOP.subtract), waits=ev)
        t6 = E("act", lambda e: e.activation(RC[:], RB[:], AF.Ln,
                                             bias=EPSC[:, 0:1], scale=1.0),
               waits=[t4, t_consts])
        t9 = E("act", lambda e: e.activation(RBC[:], RC[:], AF.Exp, scale=-0.5),
               waits=[rbc_free])
        t8 = E("dve", lambda e: e.tensor_mul(RA[:], RA[:], RBC[:]), waits=[t9])
        t10 = E("dve", lambda e: e.tensor_copy(RMBC[:], RA[:]),
                waits=[rmbc_free])
        return t9, t10

    t_xcast = [None] * NJ
    t_xsq = [None] * NJ
    pe_sq1 = [None] * NJ
    for j in range(NJ):
        if j % 2 == 0:
            k = j // 2
            tc = E("act", lambda e, k=k: e.activation(
                XT[:, k * 4096:(k + 1) * 4096], XSTP[[0, 1, 2, 0][k]][:],
                AF.Copy), waits=[t_xdma[j], t_xdma[j + 1]])
            t_xcast[j] = t_xcast[j + 1] = tc
        t_xsq[j] = E("dve", lambda e, j=j: e.tensor_mul(
            XSQ[j % 2][:], XT[:, j * 2048:(j + 1) * 2048],
            XT[:, j * 2048:(j + 1) * 2048]),
            waits=[t_xcast[j]] + ([pe_sq1[j - 2]] if j >= 2 else []))
        assert t_xsq[j] == ("dve", _sq_pos[j]), (t_xsq[j], j)
        pe_sq1[j] = ln_stats_tile(j, t_xcast[j], t_xsq[j], XT)
    stream_w("wv", 0, wvB, None, units=(0, 1))
    r1c, rm1c = [None] * NSC, [None] * NSC
    t_xn = [None] * NJ
    t_xn_c = {}
    ev1ab = []
    for sc in range(NSC):
        s0, s1 = sc * 512, (sc + 1) * 512
        ta = E("act", lambda e, sc=sc, s0=s0, s1=s1: e.activation(
            RA[:, s0:s1], PSA[sc][:, :], AF.Copy), waits=[pe_sq1[-1]])
        tb = E("act", lambda e, sc=sc, s0=s0, s1=s1: e.activation(
            RB[:, s0:s1], PSB[sc][:, :], AF.Copy), waits=[pe_sq1[-1]])
        psa_free[sc] = ta
        psb_free[sc] = tb
        ev1ab.append((ta, tb))
    def emit_rows_chunk(sc):
        s0, s1 = sc * 512, (sc + 1) * 512
        ta, tb = ev1ab[sc]
        t1 = E("dve", lambda e, s0=s0, s1=s1: e.tensor_scalar_mul(
            RA[:, s0:s1], RA[:, s0:s1], INV_D), waits=[ta])
        t3 = E("dve", lambda e, s0=s0, s1=s1: e.tensor_mul(
            RC[:, s0:s1], RA[:, s0:s1], RA[:, s0:s1]), waits=[ta])
        t4 = E("dve", lambda e, s0=s0, s1=s1: e.scalar_tensor_tensor(
            RB[:, s0:s1], RB[:, s0:s1], INV_D, RC[:, s0:s1],
            AOP.mult, AOP.subtract), waits=[ta, tb])
        t6 = E("act", lambda e, s0=s0, s1=s1: e.activation(
            RC[:, s0:s1], RB[:, s0:s1], AF.Ln, bias=EPSC[:, 0:1], scale=1.0),
            waits=[t4, t_consts])
        r1c[sc] = E("act", lambda e, s0=s0, s1=s1: e.activation(
            RBC[:, s0:s1], RC[:, s0:s1], AF.Exp, scale=-0.5))
        if sc == 0:
            stream_w("wv", 0, wvB, None, units=(2, 3))
        elif sc < 3:
            stream_w("wq", 1, wqB, None, units=(2 * (sc - 1), 2 * sc - 1))
        xn3 = XNT[:].rearrange("p (a t) -> p a t", a=NJ)[:, :, s0:s1]
        xt3 = XT[:].rearrange("p (a t) -> p a t", a=NJ)[:, :, s0:s1]
        E("dve", lambda e, xn3=xn3, xt3=xt3, s0=s0, s1=s1: e.tensor_tensor(
            xn3, xt3, RBC[:, s0:s1].rearrange("p (a t) -> p a t", a=1)
            .broadcast_to((128, NJ, s1 - s0)), AOP.mult),
            waits=[r1c[sc]] + [t_xcast[2 * k] for k in range(4)])
        t8 = E("dve", lambda e, s0=s0, s1=s1: e.tensor_mul(
            RA[:, s0:s1], RA[:, s0:s1], RBC[:, s0:s1]), waits=[r1c[sc]])
        rm1c[sc] = E("dve", lambda e, s0=s0, s1=s1: e.tensor_copy(
            RMBC[:, s0:s1], RA[:, s0:s1]))
        tsub = E("dve", lambda e, xn3=xn3, s0=s0, s1=s1: e.tensor_tensor(
            xn3, xn3, RMBC[:, s0:s1].rearrange("p (a t) -> p a t", a=1)
            .broadcast_to((128, NJ, s1 - s0)), AOP.subtract), waits=[rm1c[sc]])
        for j in range(NJ):
            t_xn_c[(j, sc)] = tsub

    emit_rows_chunk(0)


    bank = [0]

    def proj(unit_toks, slot, je, sc, rhs_sb, rhs_tok):
        z = bank[0] % 8
        bank[0] += 1
        tok = None
        pf = ps_free[z]
        seen = set()
        for jd in range(NJ):
            w_all = [unit_toks[jd // 2],
                     rhs_tok[jd] if isinstance(rhs_tok, list) else rhs_tok, pf]
            w = [t for t in w_all if t is not None and t not in seen]
            seen.update(w)
            tok = E("pe", lambda e, slot=slot, jd=jd, je=je, sc=sc, z=z: e.matmul(
                PS[z][:, :],
                WBF[slot][:, (jd * 8 + je) * 128:(jd * 8 + je + 1) * 128],
                rhs_sb[:, jd * 2048 + sc * 512: jd * 2048 + (sc + 1) * 512],
                start=(jd == 0), stop=(jd == NJ - 1)), waits=w)
        return z, tok

    w2_units = [(qq, u) for qq in range(4) for u in range(4)]
    r2k, rm2k = [None], [None]
    w2_k = [0]
    t_w2l = [None]

    def w2_unit():
        k = w2_k[0]
        w2_k[0] += 1
        qq, u = w2_units[k]
        if k == 0:
            fw = [r1]
        elif k in (8, 12):
            fw = [rm2k[0], r2k[0]]
        else:
            fw = []
        t_w2l[0] = stream_w2_unit(qq, u, fw)

    last_v_mm = [None]
    last_q_mm = [None]
    t_scan = {}
    vq_it = [0]
    for sc in range(NSC):
        for je in range(NJ):
            zv, col = je % 2, je // 2
            z, tmm = proj(t_wbf["wv"], 0, je, sc, XNT,
                          [t_xn_c[(jd, sc)] for jd in range(NJ)])
            last_v_mm[0] = tmm
            w = [tmm]
            if sc >= 1:
                w.append(t_scan[(je, sc - 1)])
            tev = E("act", lambda e, z=z, zv=zv, col=col: e.activation(
                VST[zv][:, col * 512:(col + 1) * 512], PS[z][:, :], AF.Copy),
                waits=w)
            ps_free[z] = tev
            if sc == 0:
                init = 0.0
            else:
                init = ST[:, je * 2048 + sc * 512 - 1: je * 2048 + sc * 512]
            t_scan[(je, sc)] = E("dve", lambda e, je=je, zv=zv, col=col, sc=sc,
                                 init=init: e.tensor_tensor_scan(
                ST[:, je * 2048 + sc * 512: je * 2048 + (sc + 1) * 512],
                LAM[:, je:je + 1].broadcast_to((128, 512)),
                VST[zv][:, col * 512:(col + 1) * 512], init, AOP.mult, AOP.add),
                waits=[tev, t_consts, t_scan.get((je, sc - 1))])
            vq_it[0] += 1
        if sc + 1 < NSC:
            emit_rows_chunk(sc + 1)
    r1, rm1 = r1c[-1], rm1c[-1]
    t_xn = [t_xn_c[(j, NSC - 1)] for j in range(NJ)]
    t_xn_all = t_xn[-1]

    t_y2 = {}
    for je in range(NJ):
        for sc in range(NSC):
            z, tmm = proj(t_wbf["wq"], 1, je, sc, XNT, t_xn)
            tev = E("dve", lambda e, z=z, je=je, sc=sc: e.tensor_tensor(
                ST[:, je * 2048 + sc * 512: je * 2048 + (sc + 1) * 512],
                PS[z][:, :],
                ST[:, je * 2048 + sc * 512: je * 2048 + (sc + 1) * 512], AOP.mult),
                waits=[tmm, t_scan[(je, sc)]])
            ps_free[z] = tev
            t_y2[(je, sc)] = tev
            last_q_mm[0] = tmm
            if vq_it[0] in (36, 50):
                w2_unit()
            vq_it[0] += 1
    t_y2_all = t_y2[(NJ - 1, NSC - 1)]

    stream_w("wg", 0, wgB, last_v_mm[0], cast_eng="dve")

    t_sq2 = [None] * NJ
    pe_sq2 = [None] * NJ
    for j in range(NJ):
        t_sq2[j] = E("act", lambda e, j=j: e.activation(
            XSQ[j % 2][:], ST[:, j * 2048:(j + 1) * 2048], AF.Square),
            waits=[t_y2[(j, NSC - 1)], t_xn_all, rm1]
            + ([pe_sq2[j - 2]] if j >= 2 else [pe_sq1[-1]]))
        pe_sq2[j] = ln_stats_tile(j, t_y2[(j, NSC - 1)], t_sq2[j], ST)
    ev2 = ln_stats_evict(pe_sq2[-1], [t_xn_all], [])
    r2, rm2 = ln_rows(ev2, t_xn_all, t_xn_all)
    r2k[0], rm2k[0] = r2, rm2
    t_gate = {}
    t_gate_col = [None] * NSC
    slot_last = [None] * 8
    g8 = 0
    for je in range(NJ):
        o0, o1 = je * 2048, (je + 1) * 2048
        E("dve", lambda e, o0=o0, o1=o1: e.tensor_tensor(
            ST[:, o0:o1], ST[:, o0:o1], RBC[:], AOP.mult),
            waits=[r2, pe_sq2[NJ - 1]])
        t_apl = E("dve", lambda e, o0=o0, o1=o1, je=je: e._custom_dve(
            Q_APL, out=ST[:, o0:o1], in0=ST[:, o0:o1], in1=RMBC[:],
            s0=GAM[:, je:je + 1], s1=BET[:, je:je + 1]),
            waits=[rm2, t_consts])
        for sc in range(NSC):
            zv, col = g8 % 2, (g8 // 2) % 4
            z, tmm = proj(t_wbf["wg"], 0, je, sc, XNT, t_xn)
            w = [tmm]
            if slot_last[g8 % 8] is not None:
                w.append(slot_last[g8 % 8])
            else:
                w.append(t_scan[(g8 % 8, NSC - 1)])
            tsil = E("act", lambda e, z=z, zv=zv, col=col: e.activation(
                VST[zv][:, col * 512:(col + 1) * 512], PS[z][:, :], AF.Silu),
                waits=w)
            ps_free[z] = tsil
            s0_, s1_ = je * 2048 + sc * 512, je * 2048 + (sc + 1) * 512
            t_gate[(je, sc)] = E("dve", lambda e, zv=zv, col=col, s0_=s0_, s1_=s1_:
                                 e.tensor_tensor(
                ST[:, s0_:s1_], ST[:, s0_:s1_],
                VST[zv][:, col * 512:(col + 1) * 512], AOP.mult), waits=[tsil])
            slot_last[g8 % 8] = t_gate[(je, sc)]
            g8 += 1
        if je % 2 == 1:
            stream_w("wo", 1, woB, last_q_mm[0], units=(je // 2,))
    t_gate_all = t_gate[(NJ - 1, NSC - 1)]

    t_attn = {}
    it = 0
    for sc in range(NSC):
        for ke in range(NJ):
            z = bank[0] % 8
            bank[0] += 1
            tok = None
            pf = ps_free[z]
            seen = set()
            for jd in range(NJ):
                w_all = [t_wbf["wo"][jd // 2], t_gate[(jd, sc)], pf]
                w = [t for t in w_all if t is not None and t not in seen]
                seen.update(w)
                tok = E("pe", lambda e, jd=jd, ke=ke, sc=sc, z=z: e.matmul(
                    PS[z][:, :],
                    WBF[1][:, (jd * 8 + ke) * 128:(jd * 8 + ke + 1) * 128],
                    ST[:, jd * 2048 + sc * 512: jd * 2048 + (sc + 1) * 512],
                    start=(jd == 0), stop=(jd == NJ - 1)), waits=w)
            tev = E("dve", lambda e, z=z, ke=ke, sc=sc: e.scalar_tensor_tensor(
                XT[:, ke * 2048 + sc * 512: ke * 2048 + (sc + 1) * 512],
                PS[z][:, :], 1.0,
                XT[:, ke * 2048 + sc * 512: ke * 2048 + (sc + 1) * 512],
                AOP.mult, AOP.add), waits=[tok, t_xcast[ke]])
            ps_free[z] = tev
            t_attn[(ke, sc)] = tev
            if it in (0, 6, 12, 18):
                w2_unit()
            it += 1
    t_attn_all = t_attn[(NJ - 1, NSC - 1)]

    t_out_dma = []
    outs_free = {}
    t_hq_prev = None
    slot_prev_use = {0: t_gate_all, 1: t_attn_all}
    for sh in range(NSH):
        t_hq = {}
        for qq in range(4):
            slot = qq % 2
            if f"w1_{sh}_{qq}" not in t_wbf:
                stream_w(f"w1_{sh}_{qq}", slot, w1B[qq], slot_prev_use[slot])
            for jhl in range(8):
                jh = qq * 8 + jhl
                if sh == 0 and (qq * 8 + jhl) in (1, 4, 7, 10, 13, 16, 19,
                                                   22, 25, 28):
                    w2_unit()
                for scl in range(2):
                    sc = sh * 2 + scl
                    z = bank[0] % 8
                    bank[0] += 1
                    tok = None
                    pf = ps_free[z]
                    seen = set()
                    for jd in range(NJ):
                        w_all = [t_wbf[f"w1_{sh}_{qq}"][jhl // 2], t_xn[jd], pf]
                        w = [t for t in w_all if t is not None and t not in seen]
                        seen.update(w)
                        tok = E("pe", lambda e, jd=jd, jhl=jhl, sc=sc, z=z, slot=slot: e.matmul(
                            PS[z][:, :],
                            WBF[slot][:, (jhl * 8 + jd) * 128:(jhl * 8 + jd + 1) * 128],
                            XNT[:, jd * 2048 + sc * 512: jd * 2048 + (sc + 1) * 512],
                            start=(jd == 0), stop=(jd == NJ - 1)), waits=w)
                    w2_ = [tok, t_consts]
                    if sh == 1:
                        w2_.append(t_hq_prev)
                    thq = E("dve", lambda e, z=z, jh=jh, scl=scl: e._custom_dve(
                        Q_HQ, out=HQ[:, jh * 1024 + scl * 512: jh * 1024 + (scl + 1) * 512],
                        in0=PS[z][:, :], s0=B1C[:, jh:jh + 1], s1=C_E4), waits=w2_)
                    ps_free[z] = thq
                    t_hq[(jh, scl)] = thq
            slot_prev_use[slot] = tok
        t_hq_all = t_hq[(NH - 1, 1)]
        if sh + 1 < NSH:
            for qq in range(2):
                stream_w(f"w1_{sh + 1}_{qq}", qq, w1B[qq], slot_prev_use[qq])
        pe_hq_last = None
        for scl in range(2):
            sc = sh * 2 + scl
            for k in range(NJ):
                z = bank[0] % 8
                bank[0] += 1
                tok = None
                pf = ps_free[z]
                seen = set()
                for jp in range(NH // 2):
                    w_all = [t_hq_all, t_w2l[0], t_b2q, pf, t_attn_all]
                    w = [t for t in w_all if t is not None and t not in seen]
                    seen.update(w)
                    tok = E("pe", lambda e, jp=jp, k=k, scl=scl, z=z: e.matmul(
                        PS[z][:, :],
                        W2Q[:, (jp * 8 + k) * 256:(jp * 8 + k) * 256 + 256]
                        .rearrange("p (i c) -> p i c", i=2),
                        HQ[:].rearrange("p (jh s) -> p jh s", jh=32)
                        [:, 2 * jp:2 * jp + 2, scl * 512:(scl + 1) * 512],
                        start=(jp == 0), stop=(jp == NH // 2 - 1),
                        perf_mode=mybir.MatmulPerfMode.DoubleRow), waits=w)
                pe_hq_last = tok
                oslc = (sc % 2) * 1024 + (k % 2) * 512
                tq = E("dve", lambda e, z=z, k=k, sc=sc, oslc=oslc: e._custom_dve(
                    Q_OUT, out=OUTS[:, oslc:oslc + 512],
                    in0=PS[z][:, :],
                    in1=XT[:, k * 2048 + sc * 512: k * 2048 + (sc + 1) * 512],
                    s0=B2Q[:, k:k + 1], s1=C_F16),
                    waits=[tok, t_attn[(k, sc)], outs_free.get(oslc)])
                ps_free[z] = tq
                td = E("act", lambda e, k=k, sc=sc, oslc=oslc: e.dma_start(
                    outT[k * 128:(k + 1) * 128, sc * 512:(sc + 1) * 512],
                    OUTS[:, oslc:oslc + 512]), waits=[tq], sem=f"gdo{oslc // 512}")
                t_out_dma.append(td)
                outs_free[oslc] = td
        t_hq_prev = pe_hq_last

    final_waits = [(s, p.n[s]) for s in ("gdo0", "gdo1", "gdo2", "gdo3")]

    @blk.tensor
    def _(e):
        for go in p.q["pe"]:
            go(e, sems)

    @blk.vector
    def _(e):
        for go in p.q["dve"]:
            go(e, sems)

    @blk.scalar
    def _(e):
        for go in p.q["act"]:
            go(e, sems)

    @blk.sync
    def _(e):
        for go in p.q["sd"]:
            go(e, sems)

    @blk.gpsimd
    def _(e):
        for go in p.q["gd"]:
            go(e, sems)

    p.q["dve"].append(lambda e, sems: [e.wait_ge(sems[s], v) for s, v in final_waits] and None)

    st.close()
    return nc


def _prep_inputs(L0_attn_x, L0_attn_decay, L0_attn_Wq, L0_attn_Wv, L0_attn_Wo,
                 L0_attn_Wg, L0_attn_ln_gamma, L0_attn_ln_beta,
                 L0_ffn_w1, L0_ffn_b1, L0_ffn_w2, L0_ffn_b2):
    f = np.float32

    def blk1024(W):
        return np.ascontiguousarray(
            np.asarray(W, f).T.reshape(8, 128, 8, 128).transpose(1, 0, 2, 3).reshape(128, 8192))

    w1B = np.ascontiguousarray(
        np.asarray(L0_ffn_w1, f).T.reshape(8, 128, 32, 128)
        .transpose(2, 1, 0, 3)
        .reshape(4, 8, 128, 8, 128)
        .transpose(0, 2, 1, 3, 4)
        .reshape(4, 128, 8192))
    w2B = np.ascontiguousarray(
        np.asarray(L0_ffn_w2, f).T.reshape(16, 2, 128, 8, 128)
        .transpose(2, 0, 3, 1, 4)
        .reshape(128, 4, 8192)
        .transpose(1, 0, 2))
    w2B = np.ascontiguousarray(w2B)
    shared = dict(
        wvB=blk1024(L0_attn_Wv), wqB=blk1024(L0_attn_Wq),
        wgB=blk1024(L0_attn_Wg), woB=blk1024(L0_attn_Wo),
        w1B=w1B, w2B=w2B,
        b1c=np.ascontiguousarray(np.asarray(L0_ffn_b1, f).reshape(32, 128).T),
        b2c=np.ascontiguousarray(np.asarray(L0_ffn_b2, f).reshape(8, 128).T),
        gamc=np.ascontiguousarray(np.asarray(L0_attn_ln_gamma, f).reshape(8, 128).T),
        betc=np.ascontiguousarray(np.asarray(L0_attn_ln_beta, f).reshape(8, 128).T),
        lamc=np.ascontiguousarray(np.repeat(np.asarray(L0_attn_decay, f), 64).reshape(8, 128).T),
        onesb=np.ones((128, 128), dtype=ml_dtypes.bfloat16),
        epsc=np.full((128, 1), 1e-5, dtype=f),
    )
    in_maps = []
    for c in range(8):
        m = dict(shared)
        m["xT"] = np.ascontiguousarray(np.asarray(L0_attn_x[c], f).T)
        in_maps.append(m)
    return in_maps


_NC_CACHE = {}


def kernel(**inputs):
    in_maps = _prep_inputs(**inputs)
    if "nc" not in _NC_CACHE:
        nc = bass.Bass("TRN2", target_bir_lowering=False)
        build(nc)
        mybir.codegen_inst_isa_subclasses(nc)
        _NC_CACHE["nc"] = nc
    nc = _NC_CACHE["nc"]
    res = run_bass_kernel_spmd(nc, in_maps, core_ids=list(range(8)))
    out = np.stack([np.asarray(r["outT"], dtype=np.float32).T for r in res.results])
    return np.ascontiguousarray(out)


# version 25
# speedup vs baseline: 1.0650x; 1.0650x over previous
import contextlib
import numpy as np
import ml_dtypes
import concourse.bass as bass
import concourse.mybir as mybir
from concourse.bass_utils import run_bass_kernel_spmd
from concourse.alu_op_type import AluOpType
import concourse.dve_ops as dve_ops
from concourse.dve_op import DveOp
from concourse.dve_spec import Spec, Src0, Src1, C0, C1, C2, Zero, select, maxx, relu
from concourse.dve_spec import lower as _spec_lower

F32 = mybir.dt.float32
BF16 = mybir.dt.bfloat16
FP8 = mybir.dt.float8e4
AF = mybir.ActivationFunctionType
AOP = AluOpType


def _fq(x, man, min_exp, max_val):
    m, e = np.frexp(x)
    eff = np.maximum(e - 1, min_exp)
    scale = np.ldexp(np.ones_like(x), eff - man)
    with np.errstate(invalid="ignore"):
        q = np.round(np.divide(x, scale, where=scale != 0, out=np.zeros_like(x))) * scale
    q = np.where(np.abs(q) > max_val, np.sign(x) * np.inf, q)
    return np.where(x == 0, x, q)


def _rne_sig(x, nbits):
    m, e = np.frexp(x)
    scale = np.ldexp(np.ones_like(x), e - nbits)
    with np.errstate(invalid="ignore"):
        q = np.round(np.divide(x, scale, where=scale != 0, out=np.zeros_like(x))) * scale
    return np.where(x == 0, x, q)


def _sbits(s0):
    c = np.asarray(s0).ravel()[0] if not np.isscalar(s0) else s0
    return 24 - int(round(np.log2(float(c) - 1.0)))


def _ref_qrne(in0, in1, s0, s1, imm2):
    return _rne_sig(in0, _sbits(s0)).astype(in0.dtype)


def _ref_qsubn(in0, in1, s0, s1, imm2):
    A = np.asarray(s0).ravel()[0] if not np.isscalar(s0) else s0
    G = float(A) / (1.5 * 2 ** 23)
    lo = np.round(in1 / G) * G
    q = np.where(np.abs(in1) < imm2, lo, in0)
    q = np.where(np.abs(q) > 240.0, np.sign(in1) * np.inf, q)
    return q.astype(in0.dtype)


def _ref_qhq(in0, in1, s0, s1, imm2):
    z = np.maximum(in0 + s0, 0)
    return _fq(z, 3, -6, 240.0).astype(in0.dtype)


def _ref_qout(in0, in1, s0, s1, imm2):
    s0q = _fq(np.asarray(s0, dtype=in0.dtype), 10, -14, 65504.0)
    return (_fq(in0 + s0q, 10, -14, 65504.0) + in1).astype(in0.dtype)


def _mkop(name, spec, ref, row_donor=None):
    for o in dve_ops.OPS:
        if o.name == name:
            return o
    sha3 = _spec_lower(spec, ver="v3", perf_en=False, name=name).sha("v3")
    op = DveOp(name, spec, subdim=False, uops_sha={"v3": sha3}, reference=ref)
    dve_ops.OPS.append(op)
    dve_ops.CUSTOM_DVE_SPECS[name] = spec
    dve_ops.CUSTOM_DVE_REFS[name] = ref
    if row_donor is None:
        row = max(dve_ops._SUB_OPCODE_FOR_NAME.values()) + 1
        assert row < 0x20, row
    else:
        row = dve_ops._SUB_OPCODE_FOR_NAME[row_donor]
    dve_ops._SUB_OPCODE_FOR_NAME[name] = row
    return op


_pA = Src0 * C0
Q_RNE = _mkop("Q_RNE", Spec(body=_pA + (Src0 - _pA)), _ref_qrne)
_zB = relu(Src0 + C0)
_pB = _zB * C1
Q_HQ = _mkop("Q_HQ", Spec(body=_pB + (_zB - _pB)), _ref_qhq)
_azC = maxx(Src1, Zero - Src1)
_loC = (Src1 + C0) - C0
Q_SUBN = _mkop("Q_SUBN", Spec(body=select(_azC < C2, _loC, Src0)), _ref_qsubn,
               row_donor="DVEHASH32L_SEED")
_sD = Src0 + C0
_pD = _sD * C1
Q_OUT = _mkop("Q_OUT", Spec(body=(_pD + (_sD - _pD)) + Src1), _ref_qout,
              row_donor="DVEHASH32L_BODY")


def _ref_qapl(in0, in1, s0, s1, imm2):
    return ((in0 - in1) * s0 + s1).astype(in0.dtype)


Q_APL = _mkop("Q_APL", Spec(body=(Src0 - Src1) * C0 + C1), _ref_qapl,
              row_donor="DVEHASH32L_DRAIN")

C_E4 = float(2.0 ** 20 + 1)
C_F16 = float(2.0 ** 13 + 1)
A_E4 = float(1.5 * 2 ** 23 * 2.0 ** -9)
THR_E4 = float(2.0 ** -6)

NJ, NSC, SC, NH, NSH = 8, 4, 512, 32, 2
INV_D = float(1.0 / 1024.0)
LN_EPS = 1e-5


class P:
    COMPUTE = ("pe", "dve", "act", "gp")
    DMAS = ("sdc", "sdx", "sd0", "sd1", "gd0", "gd1", "gdo0", "gdo1", "gdo2", "gdo3",
            "gax", "gaw", "gw0", "gw1",
            "gx0", "gx1", "gx2", "gx3", "gx4", "gx5", "gx6", "gx7")
    ENGS = COMPUTE + DMAS

    def __init__(self):
        self.q = {"pe": [], "dve": [], "act": [], "sd": [], "gd": []}
        self.n = {k: 0 for k in self.ENGS}

    def emit(self, eng, fn, waits=(), sem=None):
        sem = sem or eng
        inc = 16 if sem in self.DMAS else 1
        wl = {}
        if sem in ("dve", "act", "gp") and self.n[sem] > 0:
            wl[sem] = self.n[sem]
        for t in waits:
            if t is None:
                continue
            wk, wv = t
            if wv <= 0 or (wk == sem and wk in self.COMPUTE):
                continue
            if wk not in wl or wv > wl[wk]:
                wl[wk] = wv
        wl = tuple(wl.items())
        self.n[sem] += inc
        tgt = self.n[sem]

        def go(e, sems, _fn=fn, _wl=wl, _sem=sem, _inc=inc):
            for wk, wv in _wl:
                e.wait_ge(sems[wk], wv)
            _fn(e).then_inc(sems[_sem], _inc)

        self.q[eng].append(go)
        return (sem, tgt)


def build(nc):
    dt = nc.dram_tensor
    xT = dt("xT", [1024, 2048], F32, kind="ExternalInput").ap()
    wvB = dt("wvB", [128, 8192], F32, kind="ExternalInput").ap()
    wqB = dt("wqB", [128, 8192], F32, kind="ExternalInput").ap()
    wgB = dt("wgB", [128, 8192], F32, kind="ExternalInput").ap()
    woB = dt("woB", [128, 8192], F32, kind="ExternalInput").ap()
    w1B = dt("w1B", [4, 128, 8192], F32, kind="ExternalInput").ap()
    w2B = dt("w2B", [4, 128, 8192], F32, kind="ExternalInput").ap()
    b1c = dt("b1c", [128, 32], F32, kind="ExternalInput").ap()
    b2c = dt("b2c", [128, 8], F32, kind="ExternalInput").ap()
    gamc = dt("gamc", [128, 8], F32, kind="ExternalInput").ap()
    betc = dt("betc", [128, 8], F32, kind="ExternalInput").ap()
    lamc = dt("lamc", [128, 8], F32, kind="ExternalInput").ap()
    onesb = dt("onesb", [128, 128], BF16, kind="ExternalInput").ap()
    epsc = dt("epsc", [128, 1], F32, kind="ExternalInput").ap()
    outT = dt("outT", [1024, 2048], F32, kind="ExternalOutput").ap()

    st = contextlib.ExitStack()
    mk = st.enter_context
    XT = mk(nc.sbuf_tensor("XT", [128, 16384], BF16))
    XNT = mk(nc.sbuf_tensor("XNT", [128, 16384], BF16))
    WST = [mk(nc.sbuf_tensor(f"WST{i}", [128, 2048], F32)) for i in range(2)]
    WBF = [mk(nc.sbuf_tensor(f"WBF{i}", [128, 8192], BF16)) for i in range(2)]
    RBC = mk(nc.sbuf_tensor("RBC", [128, 2048], BF16))
    RMBC = mk(nc.sbuf_tensor("RMBC", [128, 2048], BF16))
    B1C = mk(nc.sbuf_tensor("B1C", [128, 32], F32))
    B2C = mk(nc.sbuf_tensor("B2C", [128, 8], F32))
    B2Q = mk(nc.sbuf_tensor("B2Q", [128, 8], F32))
    GAM = mk(nc.sbuf_tensor("GAM", [128, 8], F32))
    BET = mk(nc.sbuf_tensor("BET", [128, 8], F32))
    LAM = mk(nc.sbuf_tensor("LAM", [128, 8], F32))
    ONES = mk(nc.sbuf_tensor("ONES", [128, 128], BF16))
    EPSC = mk(nc.sbuf_tensor("EPSC", [128, 1], F32))
    TOPA = 147424
    ST = nc.alloc_sbuf_tensor_at("ST", [128, 16384], BF16, offset=TOPA)
    HQ = nc.alloc_sbuf_tensor_at("HQ", [128, 32768], FP8, offset=TOPA)
    XST = [nc.alloc_sbuf_tensor_at(f"XST{i}", [128, 2048], F32, offset=TOPA + 32768 + 8192 * i)
           for i in range(2)] + [
          nc.alloc_sbuf_tensor_at(f"XST{i + 2}", [128, 2048], F32, offset=TOPA + 8192 * i)
           for i in range(4)]
    RA = nc.alloc_sbuf_tensor_at("RA", [128, 2048], F32, offset=TOPA + 49152)
    HIQE = nc.alloc_sbuf_tensor_at("HIQE", [128, 2048], BF16, offset=TOPA - 4096)
    XSTP = [nc.alloc_sbuf_tensor_at(f"XSTP{i}", [128, 4096], F32, offset=o)
            for i, o in enumerate((TOPA, TOPA + 16384, TOPA + 32768))]
    RB = nc.alloc_sbuf_tensor_at("RB", [128, 2048], F32, offset=TOPA + 57344)
    W2Q = nc.alloc_sbuf_tensor_at("W2Q", [128, 32768], FP8, offset=TOPA + 32768)
    XSQ = [nc.alloc_sbuf_tensor_at(f"XSQ{i}", [128, 2048], BF16, offset=TOPA + 65536 + 4096 * i)
           for i in range(2)]
    HIQ = nc.alloc_sbuf_tensor_at("HIQ", [128, 2048], BF16, offset=TOPA + 65536)
    VST = [nc.alloc_sbuf_tensor_at(f"VST{i}", [128, 2048], BF16, offset=TOPA + 73728 + 4096 * i)
           for i in range(2)]
    RC = nc.alloc_sbuf_tensor_at("RC", [128, 2048], F32, offset=TOPA + 65536)
    OUTS = nc.alloc_sbuf_tensor_at("OUTS", [128, 2048], F32, offset=TOPA + 73728)
    PS = [mk(nc.psum_tensor(f"PS{i}", [128, 512], F32)) for i in range(8)]
    PSA = PS[:4]
    PSB = PS[4:]
    sems = {k: mk(nc.semaphore(f"sem_{k}")) for k in P.ENGS}
    blk = mk(nc.Block())

    p = P()
    E = p.emit

    t_consts = None
    for dst, src in ((B1C, b1c), (B2C, b2c), (GAM, gamc), (BET, betc),
                     (LAM, lamc), (ONES, onesb), (EPSC, epsc)):
        t_consts = E("sd", lambda e, d=dst, s=src: e.dma_start(d[:], s), sem="sdc")
    t_b2q = E("dve", lambda e: e._custom_dve(Q_RNE, out=B2Q[:], in0=B2C[:], s0=C_F16),
              waits=[t_consts])

    t_xdma = [None] * NJ
    _act0 = p.n["act"]
    _dve0 = p.n["dve"]
    _sq_pos = {j: _dve0 + 1 + j for j in range(NJ)}
    for j in range(NJ):
        w = [("act", _act0 + 1)] if j >= 6 else []
        qeng = ["gd", "sd"][j % 2]
        pt, hh = [0, 1, 2, 0][j // 2], j % 2
        t_xdma[j] = E(qeng, lambda e, pt=pt, hh=hh, j=j: e.dma_start(
            XSTP[pt][:, hh * 2048:(hh + 1) * 2048],
            xT[j * 128:(j + 1) * 128, :]), waits=w, sem=f"gx{j}")

    wst_free = [None, None]
    t_wbf = {}
    wu = [0]

    def stream_w(name, slot, dram_ap, free_after, cast_eng="act",
                 qpat=("sd", "sd", "sd", "sd"), units=(0, 1, 2, 3)):
        toks = t_wbf.get(name, [None] * 4)
        for u in units:
            z = wu[0] % 2
            wu[0] += 1
            if qpat[u] == "act":
                t_d = E("act", lambda e, z=z, u=u, d=dram_ap: e.dma_start(
                    WST[z][:], d[:, u * 2048:(u + 1) * 2048]), waits=[wst_free[z]],
                        sem="gaw")
            else:
                t_d = E("sd", lambda e, z=z, u=u, d=dram_ap: e.dma_start(
                    WST[z][:], d[:, u * 2048:(u + 1) * 2048]), waits=[wst_free[z]],
                        sem=f"sd{z}")
            if cast_eng == "act":
                t_c = E("act", lambda e, z=z, u=u, slot=slot: e.activation(
                    WBF[slot][:, u * 2048:(u + 1) * 2048], WST[z][:], AF.Copy),
                    waits=[t_d, free_after])
            elif cast_eng == "gp":
                t_c = E("gd", lambda e, z=z, u=u, slot=slot: e.tensor_copy(
                    WBF[slot][:, u * 2048:(u + 1) * 2048], WST[z][:]),
                    waits=[t_d, free_after], sem="gp")
            else:
                t_c = E("dve", lambda e, z=z, u=u, slot=slot: e.tensor_copy(
                    WBF[slot][:, u * 2048:(u + 1) * 2048], WST[z][:]),
                    waits=[t_d, free_after])
            wst_free[z] = t_c
            toks[u] = t_c
        t_wbf[name] = toks

    def stream_w2_unit(qq, u, first_waits):
        z = wu[0] % 2
        wu[0] += 1
        fw = list(first_waits)
        t_d = E("sd", lambda e, z=z, u=u, qq=qq: e.dma_start(
            WST[z][:], w2B[qq, :, u * 2048:(u + 1) * 2048]), waits=[wst_free[z]],
                sem=f"sd{z}")
        t_hi = E("dve", lambda e, z=z: e._custom_dve(
            Q_RNE, out=HIQE[:], in0=WST[z][:], s0=C_E4), waits=[t_d] + fw)
        t_q = E("dve", lambda e, z=z, qq=qq, u=u: e._custom_dve(
            Q_SUBN, out=W2Q[:, qq * 8192 + u * 2048: qq * 8192 + (u + 1) * 2048],
            in0=HIQE[:], in1=WST[z][:], s0=A_E4, imm2=THR_E4), waits=fw + [t_d])
        wst_free[z] = t_q
        return t_q

    ps_free = [None] * 8
    class _PFa:
        def __getitem__(self, i): return ps_free[i]
        def __setitem__(self, i, v): ps_free[i] = v
    class _PFb:
        def __getitem__(self, i): return ps_free[4 + i]
        def __setitem__(self, i, v): ps_free[4 + i] = v
    psa_free = _PFa()
    psb_free = _PFb()

    def ln_stats_tile(j, src_tok, sq_tok, src_sb):
        seen = set()
        for sc in range(NSC):
            w_all = [src_tok, t_consts, psa_free[sc]]
            w = [t for t in w_all if t is not None and t not in seen]
            seen.update(w)
            E("pe", lambda e, j=j, sc=sc, src_sb=src_sb: e.matmul(
                PSA[sc][:, :], ONES[:],
                src_sb[:, j * 2048 + sc * 512: j * 2048 + (sc + 1) * 512],
                start=(j == 0), stop=(j == NJ - 1)), waits=w)
        last_pe = None
        for sc in range(NSC):
            w_all = [sq_tok, t_consts, psb_free[sc]]
            w = [t for t in w_all if t is not None and t not in seen]
            seen.update(w)
            last_pe = E("pe", lambda e, j=j, sc=sc: e.matmul(
                PSB[sc][:, :], ONES[:], XSQ[j % 2][:, sc * 512:(sc + 1) * 512],
                start=(j == 0), stop=(j == NJ - 1)), waits=w)
        return last_pe

    def ln_stats_evict(last_pe, extra_a, extra_b):
        ev = []
        for sc in range(NSC):
            ta = E("act", lambda e, sc=sc: e.activation(
                RA[:, sc * 512:(sc + 1) * 512], PSA[sc][:, :], AF.Copy),
                waits=[last_pe] + extra_a)
            tb = E("act", lambda e, sc=sc: e.activation(
                RB[:, sc * 512:(sc + 1) * 512], PSB[sc][:, :], AF.Copy),
                waits=[last_pe] + extra_b)
            psa_free[sc] = ta
            psb_free[sc] = tb
            ev.extend([ta, tb])
        return ev

    def ln_rows(ev, rbc_free, rmbc_free):
        t1 = E("dve", lambda e: e.tensor_scalar_mul(RA[:], RA[:], INV_D), waits=ev)
        t3 = E("dve", lambda e: e.tensor_mul(RC[:], RA[:], RA[:]), waits=ev)
        t4 = E("dve", lambda e: e.scalar_tensor_tensor(RB[:], RB[:], INV_D, RC[:],
                                                       AOP.mult, AOP.subtract), waits=ev)
        t6 = E("act", lambda e: e.activation(RC[:], RB[:], AF.Ln,
                                             bias=EPSC[:, 0:1], scale=1.0),
               waits=[t4, t_consts])
        t9 = E("act", lambda e: e.activation(RBC[:], RC[:], AF.Exp, scale=-0.5),
               waits=[rbc_free])
        t8 = E("dve", lambda e: e.tensor_mul(RA[:], RA[:], RBC[:]), waits=[t9])
        t10 = E("dve", lambda e: e.tensor_copy(RMBC[:], RA[:]),
                waits=[rmbc_free])
        return t9, t10

    t_xcast = [None] * NJ
    t_xsq = [None] * NJ
    pe_sq1 = [None] * NJ
    for j in range(NJ):
        if j % 2 == 0:
            k = j // 2
            tc = E("act", lambda e, k=k: e.activation(
                XT[:, k * 4096:(k + 1) * 4096], XSTP[[0, 1, 2, 0][k]][:],
                AF.Copy), waits=[t_xdma[j], t_xdma[j + 1]])
            t_xcast[j] = t_xcast[j + 1] = tc
        t_xsq[j] = E("dve", lambda e, j=j: e.tensor_mul(
            XSQ[j % 2][:], XT[:, j * 2048:(j + 1) * 2048],
            XT[:, j * 2048:(j + 1) * 2048]),
            waits=[t_xcast[j]] + ([pe_sq1[j - 2]] if j >= 2 else []))
        assert t_xsq[j] == ("dve", _sq_pos[j]), (t_xsq[j], j)
        pe_sq1[j] = ln_stats_tile(j, t_xcast[j], t_xsq[j], XT)
    stream_w("wv", 0, wvB, None, units=(0, 1))
    r1c, rm1c = [None] * NSC, [None] * NSC
    t_xn = [None] * NJ
    t_xn_c = {}
    ev1ab = []
    for sc in range(NSC):
        s0, s1 = sc * 512, (sc + 1) * 512
        ta = E("act", lambda e, sc=sc, s0=s0, s1=s1: e.activation(
            RA[:, s0:s1], PSA[sc][:, :], AF.Copy), waits=[pe_sq1[-1]])
        tb = E("act", lambda e, sc=sc, s0=s0, s1=s1: e.activation(
            RB[:, s0:s1], PSB[sc][:, :], AF.Copy), waits=[pe_sq1[-1]])
        psa_free[sc] = ta
        psb_free[sc] = tb
        ev1ab.append((ta, tb))
    for sc in range(NSC):
        s0, s1 = sc * 512, (sc + 1) * 512
        ta, tb = ev1ab[sc]
        t1 = E("dve", lambda e, s0=s0, s1=s1: e.tensor_scalar_mul(
            RA[:, s0:s1], RA[:, s0:s1], INV_D), waits=[ta])
        t3 = E("dve", lambda e, s0=s0, s1=s1: e.tensor_mul(
            RC[:, s0:s1], RA[:, s0:s1], RA[:, s0:s1]), waits=[ta])
        t4 = E("dve", lambda e, s0=s0, s1=s1: e.scalar_tensor_tensor(
            RB[:, s0:s1], RB[:, s0:s1], INV_D, RC[:, s0:s1],
            AOP.mult, AOP.subtract), waits=[ta, tb])
        t6 = E("act", lambda e, s0=s0, s1=s1: e.activation(
            RC[:, s0:s1], RB[:, s0:s1], AF.Ln, bias=EPSC[:, 0:1], scale=1.0),
            waits=[t4, t_consts])
        r1c[sc] = E("act", lambda e, s0=s0, s1=s1: e.activation(
            RBC[:, s0:s1], RC[:, s0:s1], AF.Exp, scale=-0.5))
        if sc == 0:
            stream_w("wv", 0, wvB, None, units=(2, 3))
        elif sc < 3:
            stream_w("wq", 1, wqB, None, units=(2 * (sc - 1), 2 * sc - 1))
        xn3 = XNT[:].rearrange("p (a t) -> p a t", a=NJ)[:, :, s0:s1]
        xt3 = XT[:].rearrange("p (a t) -> p a t", a=NJ)[:, :, s0:s1]
        E("dve", lambda e, xn3=xn3, xt3=xt3, s0=s0, s1=s1: e.tensor_tensor(
            xn3, xt3, RBC[:, s0:s1].rearrange("p (a t) -> p a t", a=1)
            .broadcast_to((128, NJ, s1 - s0)), AOP.mult),
            waits=[r1c[sc]] + [t_xcast[2 * k] for k in range(4)])
        t8 = E("dve", lambda e, s0=s0, s1=s1: e.tensor_mul(
            RA[:, s0:s1], RA[:, s0:s1], RBC[:, s0:s1]), waits=[r1c[sc]])
        rm1c[sc] = E("dve", lambda e, s0=s0, s1=s1: e.tensor_copy(
            RMBC[:, s0:s1], RA[:, s0:s1]))
        tsub = E("dve", lambda e, xn3=xn3, s0=s0, s1=s1: e.tensor_tensor(
            xn3, xn3, RMBC[:, s0:s1].rearrange("p (a t) -> p a t", a=1)
            .broadcast_to((128, NJ, s1 - s0)), AOP.subtract), waits=[rm1c[sc]])
        for j in range(NJ):
            t_xn_c[(j, sc)] = tsub
    r1, rm1 = r1c[-1], rm1c[-1]
    for j in range(NJ):
        t_xn[j] = t_xn_c[(j, NSC - 1)]
    t_xn_all = t_xn[-1]


    bank = [0]

    def proj(unit_toks, slot, je, sc, rhs_sb, rhs_tok):
        z = bank[0] % 8
        bank[0] += 1
        tok = None
        pf = ps_free[z]
        seen = set()
        for jd in range(NJ):
            w_all = [unit_toks[jd // 2],
                     rhs_tok[jd] if isinstance(rhs_tok, list) else rhs_tok, pf]
            w = [t for t in w_all if t is not None and t not in seen]
            seen.update(w)
            tok = E("pe", lambda e, slot=slot, jd=jd, je=je, sc=sc, z=z: e.matmul(
                PS[z][:, :],
                WBF[slot][:, (jd * 8 + je) * 128:(jd * 8 + je + 1) * 128],
                rhs_sb[:, jd * 2048 + sc * 512: jd * 2048 + (sc + 1) * 512],
                start=(jd == 0), stop=(jd == NJ - 1)), waits=w)
        return z, tok

    w2_units = [(qq, u) for qq in range(4) for u in range(4)]
    r2k, rm2k = [None], [None]
    w2_k = [0]
    t_w2l = [None]

    def w2_unit():
        k = w2_k[0]
        w2_k[0] += 1
        qq, u = w2_units[k]
        if k == 0:
            fw = [r1]
        elif k in (8, 12):
            fw = [rm2k[0], r2k[0]]
        else:
            fw = []
        t_w2l[0] = stream_w2_unit(qq, u, fw)

    last_v_mm = [None]
    last_q_mm = [None]
    t_scan = {}
    vq_it = [0]
    for sc in range(NSC):
        for je in range(NJ):
            zv = je % 2
            z, tmm = proj(t_wbf["wv"], 0, je, sc, XNT,
                          [t_xn_c[(jd, sc)] for jd in range(NJ)])
            last_v_mm[0] = tmm
            w = [tmm]
            if je >= 2:
                w.append(t_scan[(je - 2, sc)])
            tev = E("act", lambda e, z=z, zv=zv, sc=sc: e.activation(
                VST[zv][:, sc * 512:(sc + 1) * 512], PS[z][:, :], AF.Copy), waits=w)
            ps_free[z] = tev
            if sc == 0:
                init = 0.0
            else:
                init = ST[:, je * 2048 + sc * 512 - 1: je * 2048 + sc * 512]
            t_scan[(je, sc)] = E("dve", lambda e, je=je, zv=zv, sc=sc, init=init:
                                 e.tensor_tensor_scan(
                ST[:, je * 2048 + sc * 512: je * 2048 + (sc + 1) * 512],
                LAM[:, je:je + 1].broadcast_to((128, 512)),
                VST[zv][:, sc * 512:(sc + 1) * 512], init, AOP.mult, AOP.add),
                waits=[tev, t_consts, t_scan.get((je, sc - 1))])
            vq_it[0] += 1

    t_y2 = {}
    for je in range(NJ):
        for sc in range(NSC):
            z, tmm = proj(t_wbf["wq"], 1, je, sc, XNT, t_xn)
            tev = E("dve", lambda e, z=z, je=je, sc=sc: e.tensor_tensor(
                ST[:, je * 2048 + sc * 512: je * 2048 + (sc + 1) * 512],
                PS[z][:, :],
                ST[:, je * 2048 + sc * 512: je * 2048 + (sc + 1) * 512], AOP.mult),
                waits=[tmm, t_scan[(je, sc)]])
            ps_free[z] = tev
            t_y2[(je, sc)] = tev
            last_q_mm[0] = tmm
            if vq_it[0] in (34, 39, 44, 49):
                w2_unit()
            vq_it[0] += 1
    t_y2_all = t_y2[(NJ - 1, NSC - 1)]

    stream_w("wg", 0, wgB, last_v_mm[0], cast_eng="dve")

    t_sq2 = [None] * NJ
    pe_sq2 = [None] * NJ
    for j in range(NJ):
        t_sq2[j] = E("act", lambda e, j=j: e.activation(
            XSQ[j % 2][:], ST[:, j * 2048:(j + 1) * 2048], AF.Square),
            waits=[t_y2[(j, NSC - 1)], t_xn_all, rm1]
            + ([pe_sq2[j - 2]] if j >= 2 else [pe_sq1[-1]]))
        pe_sq2[j] = ln_stats_tile(j, t_y2[(j, NSC - 1)], t_sq2[j], ST)
    ev2 = ln_stats_evict(pe_sq2[-1], [t_xn_all], [])
    r2, rm2 = ln_rows(ev2, t_xn_all, t_xn_all)
    r2k[0], rm2k[0] = r2, rm2
    t_gate = {}
    t_gate_col = [None] * NSC
    slot_last = [None] * 8
    g8 = 0
    for je in range(NJ):
        o0, o1 = je * 2048, (je + 1) * 2048
        E("dve", lambda e, o0=o0, o1=o1: e.tensor_tensor(
            ST[:, o0:o1], ST[:, o0:o1], RBC[:], AOP.mult),
            waits=[r2, pe_sq2[NJ - 1]])
        t_apl = E("dve", lambda e, o0=o0, o1=o1, je=je: e._custom_dve(
            Q_APL, out=ST[:, o0:o1], in0=ST[:, o0:o1], in1=RMBC[:],
            s0=GAM[:, je:je + 1], s1=BET[:, je:je + 1]),
            waits=[rm2, t_consts])
        for sc in range(NSC):
            zv, col = g8 % 2, (g8 // 2) % 4
            z, tmm = proj(t_wbf["wg"], 0, je, sc, XNT, t_xn)
            w = [tmm]
            if slot_last[g8 % 8] is not None:
                w.append(slot_last[g8 % 8])
            else:
                w.append(t_scan[(6 + zv, col)])
            tsil = E("act", lambda e, z=z, zv=zv, col=col: e.activation(
                VST[zv][:, col * 512:(col + 1) * 512], PS[z][:, :], AF.Silu),
                waits=w)
            ps_free[z] = tsil
            s0_, s1_ = je * 2048 + sc * 512, je * 2048 + (sc + 1) * 512
            t_gate[(je, sc)] = E("dve", lambda e, zv=zv, col=col, s0_=s0_, s1_=s1_:
                                 e.tensor_tensor(
                ST[:, s0_:s1_], ST[:, s0_:s1_],
                VST[zv][:, col * 512:(col + 1) * 512], AOP.mult), waits=[tsil])
            slot_last[g8 % 8] = t_gate[(je, sc)]
            g8 += 1
        if je % 2 == 1:
            stream_w("wo", 1, woB, last_q_mm[0], units=(je // 2,))
    t_gate_all = t_gate[(NJ - 1, NSC - 1)]

    t_attn = {}
    it = 0
    for sc in range(NSC):
        for ke in range(NJ):
            z = bank[0] % 8
            bank[0] += 1
            tok = None
            pf = ps_free[z]
            seen = set()
            for jd in range(NJ):
                w_all = [t_wbf["wo"][jd // 2], t_gate[(jd, sc)], pf]
                w = [t for t in w_all if t is not None and t not in seen]
                seen.update(w)
                tok = E("pe", lambda e, jd=jd, ke=ke, sc=sc, z=z: e.matmul(
                    PS[z][:, :],
                    WBF[1][:, (jd * 8 + ke) * 128:(jd * 8 + ke + 1) * 128],
                    ST[:, jd * 2048 + sc * 512: jd * 2048 + (sc + 1) * 512],
                    start=(jd == 0), stop=(jd == NJ - 1)), waits=w)
            tev = E("dve", lambda e, z=z, ke=ke, sc=sc: e.scalar_tensor_tensor(
                XT[:, ke * 2048 + sc * 512: ke * 2048 + (sc + 1) * 512],
                PS[z][:, :], 1.0,
                XT[:, ke * 2048 + sc * 512: ke * 2048 + (sc + 1) * 512],
                AOP.mult, AOP.add), waits=[tok, t_xcast[ke]])
            ps_free[z] = tev
            t_attn[(ke, sc)] = tev
            if it in (0, 6, 12, 18):
                w2_unit()
            it += 1
    t_attn_all = t_attn[(NJ - 1, NSC - 1)]

    t_out_dma = []
    outs_free = {}
    t_hq_prev = None
    slot_prev_use = {0: t_gate_all, 1: t_attn_all}
    for sh in range(NSH):
        t_hq = {}
        for qq in range(4):
            slot = qq % 2
            if f"w1_{sh}_{qq}" not in t_wbf:
                stream_w(f"w1_{sh}_{qq}", slot, w1B[qq], slot_prev_use[slot])
            for jhl in range(8):
                jh = qq * 8 + jhl
                if sh == 0 and (qq * 8 + jhl) in (1, 4, 8, 12, 16, 20, 24, 28):
                    w2_unit()
                for scl in range(2):
                    sc = sh * 2 + scl
                    z = bank[0] % 8
                    bank[0] += 1
                    tok = None
                    pf = ps_free[z]
                    seen = set()
                    for jd in range(NJ):
                        w_all = [t_wbf[f"w1_{sh}_{qq}"][jhl // 2], t_xn[jd], pf]
                        w = [t for t in w_all if t is not None and t not in seen]
                        seen.update(w)
                        tok = E("pe", lambda e, jd=jd, jhl=jhl, sc=sc, z=z, slot=slot: e.matmul(
                            PS[z][:, :],
                            WBF[slot][:, (jhl * 8 + jd) * 128:(jhl * 8 + jd + 1) * 128],
                            XNT[:, jd * 2048 + sc * 512: jd * 2048 + (sc + 1) * 512],
                            start=(jd == 0), stop=(jd == NJ - 1)), waits=w)
                    w2_ = [tok, t_consts]
                    if sh == 1:
                        w2_.append(t_hq_prev)
                    thq = E("dve", lambda e, z=z, jh=jh, scl=scl: e._custom_dve(
                        Q_HQ, out=HQ[:, jh * 1024 + scl * 512: jh * 1024 + (scl + 1) * 512],
                        in0=PS[z][:, :], s0=B1C[:, jh:jh + 1], s1=C_E4), waits=w2_)
                    ps_free[z] = thq
                    t_hq[(jh, scl)] = thq
            slot_prev_use[slot] = tok
        t_hq_all = t_hq[(NH - 1, 1)]
        if sh + 1 < NSH:
            for qq in range(2):
                stream_w(f"w1_{sh + 1}_{qq}", qq, w1B[qq], slot_prev_use[qq])
        pe_hq_last = None
        for scl in range(2):
            sc = sh * 2 + scl
            for k in range(NJ):
                z = bank[0] % 8
                bank[0] += 1
                tok = None
                pf = ps_free[z]
                seen = set()
                for jp in range(NH // 2):
                    w_all = [t_hq_all, t_w2l[0], t_b2q, pf, t_attn_all]
                    w = [t for t in w_all if t is not None and t not in seen]
                    seen.update(w)
                    tok = E("pe", lambda e, jp=jp, k=k, scl=scl, z=z: e.matmul(
                        PS[z][:, :],
                        W2Q[:, (jp * 8 + k) * 256:(jp * 8 + k) * 256 + 256]
                        .rearrange("p (i c) -> p i c", i=2),
                        HQ[:].rearrange("p (jh s) -> p jh s", jh=32)
                        [:, 2 * jp:2 * jp + 2, scl * 512:(scl + 1) * 512],
                        start=(jp == 0), stop=(jp == NH // 2 - 1),
                        perf_mode=mybir.MatmulPerfMode.DoubleRow), waits=w)
                pe_hq_last = tok
                oslc = (sc % 2) * 1024 + (k % 2) * 512
                tq = E("dve", lambda e, z=z, k=k, sc=sc, oslc=oslc: e._custom_dve(
                    Q_OUT, out=OUTS[:, oslc:oslc + 512],
                    in0=PS[z][:, :],
                    in1=XT[:, k * 2048 + sc * 512: k * 2048 + (sc + 1) * 512],
                    s0=B2Q[:, k:k + 1], s1=C_F16),
                    waits=[tok, t_attn[(k, sc)], outs_free.get(oslc)])
                ps_free[z] = tq
                td = E("act", lambda e, k=k, sc=sc, oslc=oslc: e.dma_start(
                    outT[k * 128:(k + 1) * 128, sc * 512:(sc + 1) * 512],
                    OUTS[:, oslc:oslc + 512]), waits=[tq], sem=f"gdo{oslc // 512}")
                t_out_dma.append(td)
                outs_free[oslc] = td
        t_hq_prev = pe_hq_last

    final_waits = [(s, p.n[s]) for s in ("gdo0", "gdo1", "gdo2", "gdo3")]

    @blk.tensor
    def _(e):
        for go in p.q["pe"]:
            go(e, sems)

    @blk.vector
    def _(e):
        for go in p.q["dve"]:
            go(e, sems)

    @blk.scalar
    def _(e):
        for go in p.q["act"]:
            go(e, sems)

    @blk.sync
    def _(e):
        for go in p.q["sd"]:
            go(e, sems)

    @blk.gpsimd
    def _(e):
        for go in p.q["gd"]:
            go(e, sems)

    p.q["dve"].append(lambda e, sems: [e.wait_ge(sems[s], v) for s, v in final_waits] and None)

    st.close()
    return nc


def _prep_inputs(L0_attn_x, L0_attn_decay, L0_attn_Wq, L0_attn_Wv, L0_attn_Wo,
                 L0_attn_Wg, L0_attn_ln_gamma, L0_attn_ln_beta,
                 L0_ffn_w1, L0_ffn_b1, L0_ffn_w2, L0_ffn_b2):
    f = np.float32

    def blk1024(W):
        return np.ascontiguousarray(
            np.asarray(W, f).T.reshape(8, 128, 8, 128).transpose(1, 0, 2, 3).reshape(128, 8192))

    w1B = np.ascontiguousarray(
        np.asarray(L0_ffn_w1, f).T.reshape(8, 128, 32, 128)
        .transpose(2, 1, 0, 3)
        .reshape(4, 8, 128, 8, 128)
        .transpose(0, 2, 1, 3, 4)
        .reshape(4, 128, 8192))
    w2B = np.ascontiguousarray(
        np.asarray(L0_ffn_w2, f).T.reshape(16, 2, 128, 8, 128)
        .transpose(2, 0, 3, 1, 4)
        .reshape(128, 4, 8192)
        .transpose(1, 0, 2))
    w2B = np.ascontiguousarray(w2B)
    shared = dict(
        wvB=blk1024(L0_attn_Wv), wqB=blk1024(L0_attn_Wq),
        wgB=blk1024(L0_attn_Wg), woB=blk1024(L0_attn_Wo),
        w1B=w1B, w2B=w2B,
        b1c=np.ascontiguousarray(np.asarray(L0_ffn_b1, f).reshape(32, 128).T),
        b2c=np.ascontiguousarray(np.asarray(L0_ffn_b2, f).reshape(8, 128).T),
        gamc=np.ascontiguousarray(np.asarray(L0_attn_ln_gamma, f).reshape(8, 128).T),
        betc=np.ascontiguousarray(np.asarray(L0_attn_ln_beta, f).reshape(8, 128).T),
        lamc=np.ascontiguousarray(np.repeat(np.asarray(L0_attn_decay, f), 64).reshape(8, 128).T),
        onesb=np.ones((128, 128), dtype=ml_dtypes.bfloat16),
        epsc=np.full((128, 1), 1e-5, dtype=f),
    )
    in_maps = []
    for c in range(8):
        m = dict(shared)
        m["xT"] = np.ascontiguousarray(np.asarray(L0_attn_x[c], f).T)
        in_maps.append(m)
    return in_maps


_NC_CACHE = {}


def kernel(**inputs):
    in_maps = _prep_inputs(**inputs)
    if "nc" not in _NC_CACHE:
        nc = bass.Bass("TRN2", target_bir_lowering=False)
        build(nc)
        mybir.codegen_inst_isa_subclasses(nc)
        _NC_CACHE["nc"] = nc
    nc = _NC_CACHE["nc"]
    res = run_bass_kernel_spmd(nc, in_maps, core_ids=list(range(8)))
    out = np.stack([np.asarray(r["outT"], dtype=np.float32).T for r in res.results])
    return np.ascontiguousarray(out)


# version 26
# speedup vs baseline: 1.0714x; 1.0060x over previous
import contextlib
import numpy as np
import ml_dtypes
import concourse.bass as bass
import concourse.mybir as mybir
from concourse.bass_utils import run_bass_kernel_spmd
from concourse.alu_op_type import AluOpType
import concourse.dve_ops as dve_ops
from concourse.dve_op import DveOp
from concourse.dve_spec import Spec, Src0, Src1, C0, C1, C2, Zero, select, maxx, relu
from concourse.dve_spec import lower as _spec_lower

F32 = mybir.dt.float32
BF16 = mybir.dt.bfloat16
FP8 = mybir.dt.float8e4
AF = mybir.ActivationFunctionType
AOP = AluOpType


def _fq(x, man, min_exp, max_val):
    m, e = np.frexp(x)
    eff = np.maximum(e - 1, min_exp)
    scale = np.ldexp(np.ones_like(x), eff - man)
    with np.errstate(invalid="ignore"):
        q = np.round(np.divide(x, scale, where=scale != 0, out=np.zeros_like(x))) * scale
    q = np.where(np.abs(q) > max_val, np.sign(x) * np.inf, q)
    return np.where(x == 0, x, q)


def _rne_sig(x, nbits):
    m, e = np.frexp(x)
    scale = np.ldexp(np.ones_like(x), e - nbits)
    with np.errstate(invalid="ignore"):
        q = np.round(np.divide(x, scale, where=scale != 0, out=np.zeros_like(x))) * scale
    return np.where(x == 0, x, q)


def _sbits(s0):
    c = np.asarray(s0).ravel()[0] if not np.isscalar(s0) else s0
    return 24 - int(round(np.log2(float(c) - 1.0)))


def _ref_qrne(in0, in1, s0, s1, imm2):
    return _rne_sig(in0, _sbits(s0)).astype(in0.dtype)


def _ref_qsubn(in0, in1, s0, s1, imm2):
    A = np.asarray(s0).ravel()[0] if not np.isscalar(s0) else s0
    G = float(A) / (1.5 * 2 ** 23)
    lo = np.round(in1 / G) * G
    q = np.where(np.abs(in1) < imm2, lo, in0)
    q = np.where(np.abs(q) > 240.0, np.sign(in1) * np.inf, q)
    return q.astype(in0.dtype)


def _ref_qhq(in0, in1, s0, s1, imm2):
    z = np.maximum(in0 + s0, 0)
    return _fq(z, 3, -6, 240.0).astype(in0.dtype)


def _ref_qout(in0, in1, s0, s1, imm2):
    s0q = _fq(np.asarray(s0, dtype=in0.dtype), 10, -14, 65504.0)
    return (_fq(in0 + s0q, 10, -14, 65504.0) + in1).astype(in0.dtype)


def _mkop(name, spec, ref, row_donor=None):
    for o in dve_ops.OPS:
        if o.name == name:
            return o
    sha3 = _spec_lower(spec, ver="v3", perf_en=False, name=name).sha("v3")
    op = DveOp(name, spec, subdim=False, uops_sha={"v3": sha3}, reference=ref)
    dve_ops.OPS.append(op)
    dve_ops.CUSTOM_DVE_SPECS[name] = spec
    dve_ops.CUSTOM_DVE_REFS[name] = ref
    if row_donor is None:
        row = max(dve_ops._SUB_OPCODE_FOR_NAME.values()) + 1
        assert row < 0x20, row
    else:
        row = dve_ops._SUB_OPCODE_FOR_NAME[row_donor]
    dve_ops._SUB_OPCODE_FOR_NAME[name] = row
    return op


_pA = Src0 * C0
Q_RNE = _mkop("Q_RNE", Spec(body=_pA + (Src0 - _pA)), _ref_qrne)
_zB = relu(Src0 + C0)
_pB = _zB * C1
Q_HQ = _mkop("Q_HQ", Spec(body=_pB + (_zB - _pB)), _ref_qhq)
_azC = maxx(Src1, Zero - Src1)
_loC = (Src1 + C0) - C0
Q_SUBN = _mkop("Q_SUBN", Spec(body=select(_azC < C2, _loC, Src0)), _ref_qsubn,
               row_donor="DVEHASH32L_SEED")
_sD = Src0 + C0
_pD = _sD * C1
Q_OUT = _mkop("Q_OUT", Spec(body=(_pD + (_sD - _pD)) + Src1), _ref_qout,
              row_donor="DVEHASH32L_BODY")


def _ref_qapl(in0, in1, s0, s1, imm2):
    return ((in0 - in1) * s0 + s1).astype(in0.dtype)


Q_APL = _mkop("Q_APL", Spec(body=(Src0 - Src1) * C0 + C1), _ref_qapl,
              row_donor="DVEHASH32L_DRAIN")

C_E4 = float(2.0 ** 20 + 1)
C_F16 = float(2.0 ** 13 + 1)
A_E4 = float(1.5 * 2 ** 23 * 2.0 ** -9)
THR_E4 = float(2.0 ** -6)

NJ, NSC, SC, NH, NSH = 8, 4, 512, 32, 2
INV_D = float(1.0 / 1024.0)
LN_EPS = 1e-5


class P:
    COMPUTE = ("pe", "dve", "act", "gp")
    DMAS = ("sdc", "sdx", "sd0", "sd1", "gd0", "gd1", "gdo0", "gdo1", "gdo2", "gdo3",
            "gax", "gaw", "gw0", "gw1",
            "gx0", "gx1", "gx2", "gx3", "gx4", "gx5", "gx6", "gx7")
    ENGS = COMPUTE + DMAS

    def __init__(self):
        self.q = {"pe": [], "dve": [], "act": [], "sd": [], "gd": []}
        self.n = {k: 0 for k in self.ENGS}

    def emit(self, eng, fn, waits=(), sem=None):
        sem = sem or eng
        inc = 16 if sem in self.DMAS else 1
        wl = {}
        if sem in ("dve", "act", "gp") and self.n[sem] > 0:
            wl[sem] = self.n[sem]
        for t in waits:
            if t is None:
                continue
            wk, wv = t
            if wv <= 0 or (wk == sem and wk in self.COMPUTE):
                continue
            if wk not in wl or wv > wl[wk]:
                wl[wk] = wv
        wl = tuple(wl.items())
        self.n[sem] += inc
        tgt = self.n[sem]

        def go(e, sems, _fn=fn, _wl=wl, _sem=sem, _inc=inc):
            for wk, wv in _wl:
                e.wait_ge(sems[wk], wv)
            _fn(e).then_inc(sems[_sem], _inc)

        self.q[eng].append(go)
        return (sem, tgt)


def build(nc):
    dt = nc.dram_tensor
    xT = dt("xT", [1024, 2048], F32, kind="ExternalInput").ap()
    wvB = dt("wvB", [128, 8192], F32, kind="ExternalInput").ap()
    wqB = dt("wqB", [128, 8192], F32, kind="ExternalInput").ap()
    wgB = dt("wgB", [128, 8192], F32, kind="ExternalInput").ap()
    woB = dt("woB", [128, 8192], F32, kind="ExternalInput").ap()
    w1B = dt("w1B", [4, 128, 8192], F32, kind="ExternalInput").ap()
    w2B = dt("w2B", [4, 128, 8192], F32, kind="ExternalInput").ap()
    b1c = dt("b1c", [128, 32], F32, kind="ExternalInput").ap()
    b2c = dt("b2c", [128, 8], F32, kind="ExternalInput").ap()
    gamc = dt("gamc", [128, 8], F32, kind="ExternalInput").ap()
    betc = dt("betc", [128, 8], F32, kind="ExternalInput").ap()
    lamc = dt("lamc", [128, 8], F32, kind="ExternalInput").ap()
    onesb = dt("onesb", [128, 128], BF16, kind="ExternalInput").ap()
    epsc = dt("epsc", [128, 1], F32, kind="ExternalInput").ap()
    outT = dt("outT", [1024, 2048], F32, kind="ExternalOutput").ap()

    st = contextlib.ExitStack()
    mk = st.enter_context
    XT = mk(nc.sbuf_tensor("XT", [128, 16384], BF16))
    XNT = mk(nc.sbuf_tensor("XNT", [128, 16384], BF16))
    WST = [mk(nc.sbuf_tensor(f"WST{i}", [128, 2048], F32)) for i in range(2)]
    WBF = [mk(nc.sbuf_tensor(f"WBF{i}", [128, 8192], BF16)) for i in range(2)]
    RBC = mk(nc.sbuf_tensor("RBC", [128, 2048], BF16))
    RMBC = mk(nc.sbuf_tensor("RMBC", [128, 2048], BF16))
    B1C = mk(nc.sbuf_tensor("B1C", [128, 32], F32))
    B2C = mk(nc.sbuf_tensor("B2C", [128, 8], F32))
    B2Q = mk(nc.sbuf_tensor("B2Q", [128, 8], F32))
    GAM = mk(nc.sbuf_tensor("GAM", [128, 8], F32))
    BET = mk(nc.sbuf_tensor("BET", [128, 8], F32))
    LAM = mk(nc.sbuf_tensor("LAM", [128, 8], F32))
    ONES = mk(nc.sbuf_tensor("ONES", [128, 128], BF16))
    EPSC = mk(nc.sbuf_tensor("EPSC", [128, 1], F32))
    TOPA = 147424
    ST = nc.alloc_sbuf_tensor_at("ST", [128, 16384], BF16, offset=TOPA)
    HQ = nc.alloc_sbuf_tensor_at("HQ", [128, 32768], FP8, offset=TOPA)
    XST = [nc.alloc_sbuf_tensor_at(f"XST{i}", [128, 2048], F32, offset=TOPA + 32768 + 8192 * i)
           for i in range(2)] + [
          nc.alloc_sbuf_tensor_at(f"XST{i + 2}", [128, 2048], F32, offset=TOPA + 8192 * i)
           for i in range(4)]
    RA = nc.alloc_sbuf_tensor_at("RA", [128, 2048], F32, offset=TOPA + 49152)
    HIQE = nc.alloc_sbuf_tensor_at("HIQE", [128, 2048], BF16, offset=TOPA - 4096)
    XSTP = [nc.alloc_sbuf_tensor_at(f"XSTP{i}", [128, 4096], F32, offset=o)
            for i, o in enumerate((TOPA, TOPA + 16384, TOPA + 32768))]
    RB = nc.alloc_sbuf_tensor_at("RB", [128, 2048], F32, offset=TOPA + 57344)
    W2Q = nc.alloc_sbuf_tensor_at("W2Q", [128, 32768], FP8, offset=TOPA + 32768)
    XSQ = [nc.alloc_sbuf_tensor_at(f"XSQ{i}", [128, 2048], BF16, offset=TOPA + 65536 + 4096 * i)
           for i in range(2)]
    HIQ = nc.alloc_sbuf_tensor_at("HIQ", [128, 2048], BF16, offset=TOPA + 65536)
    VST = [nc.alloc_sbuf_tensor_at(f"VST{i}", [128, 2048], BF16, offset=TOPA + 73728 + 4096 * i)
           for i in range(2)]
    RC = nc.alloc_sbuf_tensor_at("RC", [128, 2048], F32, offset=TOPA + 65536)
    OUTS = nc.alloc_sbuf_tensor_at("OUTS", [128, 2048], F32, offset=TOPA + 73728)
    PS = [mk(nc.psum_tensor(f"PS{i}", [128, 512], F32)) for i in range(8)]
    PSA = PS[:4]
    PSB = PS[4:]
    sems = {k: mk(nc.semaphore(f"sem_{k}")) for k in P.ENGS}
    blk = mk(nc.Block())

    p = P()
    E = p.emit

    t_consts = None
    for dst, src in ((B1C, b1c), (B2C, b2c), (GAM, gamc), (BET, betc),
                     (LAM, lamc), (ONES, onesb), (EPSC, epsc)):
        t_consts = E("sd", lambda e, d=dst, s=src: e.dma_start(d[:], s), sem="sdc")
    t_b2q = E("dve", lambda e: e._custom_dve(Q_RNE, out=B2Q[:], in0=B2C[:], s0=C_F16),
              waits=[t_consts])

    t_xdma = [None] * NJ
    _act0 = p.n["act"]
    _dve0 = p.n["dve"]
    _sq_pos = {j: _dve0 + 1 + j for j in range(NJ)}
    for j in range(NJ):
        w = [("act", _act0 + 1)] if j >= 6 else []
        qeng = ["gd", "sd"][j % 2]
        pt, hh = [0, 1, 2, 0][j // 2], j % 2
        t_xdma[j] = E(qeng, lambda e, pt=pt, hh=hh, j=j: e.dma_start(
            XSTP[pt][:, hh * 2048:(hh + 1) * 2048],
            xT[j * 128:(j + 1) * 128, :]), waits=w, sem=f"gx{j}")

    wst_free = [None, None]
    t_wbf = {}
    wu = [0]

    def stream_w(name, slot, dram_ap, free_after, cast_eng="act",
                 qpat=("sd", "sd", "sd", "sd"), units=(0, 1, 2, 3)):
        toks = t_wbf.get(name, [None] * 4)
        for u in units:
            z = wu[0] % 2
            wu[0] += 1
            if qpat[u] == "act":
                t_d = E("act", lambda e, z=z, u=u, d=dram_ap: e.dma_start(
                    WST[z][:], d[:, u * 2048:(u + 1) * 2048]), waits=[wst_free[z]],
                        sem="gaw")
            else:
                t_d = E("sd", lambda e, z=z, u=u, d=dram_ap: e.dma_start(
                    WST[z][:], d[:, u * 2048:(u + 1) * 2048]), waits=[wst_free[z]],
                        sem=f"sd{z}")
            if cast_eng == "act":
                t_c = E("act", lambda e, z=z, u=u, slot=slot: e.activation(
                    WBF[slot][:, u * 2048:(u + 1) * 2048], WST[z][:], AF.Copy),
                    waits=[t_d, free_after])
            elif cast_eng == "gp":
                t_c = E("gd", lambda e, z=z, u=u, slot=slot: e.tensor_copy(
                    WBF[slot][:, u * 2048:(u + 1) * 2048], WST[z][:]),
                    waits=[t_d, free_after], sem="gp")
            else:
                t_c = E("dve", lambda e, z=z, u=u, slot=slot: e.tensor_copy(
                    WBF[slot][:, u * 2048:(u + 1) * 2048], WST[z][:]),
                    waits=[t_d, free_after])
            wst_free[z] = t_c
            toks[u] = t_c
        t_wbf[name] = toks

    def stream_w2_unit(qq, u, first_waits):
        z = wu[0] % 2
        wu[0] += 1
        fw = list(first_waits)
        t_d = E("sd", lambda e, z=z, u=u, qq=qq: e.dma_start(
            WST[z][:], w2B[qq, :, u * 2048:(u + 1) * 2048]), waits=[wst_free[z]],
                sem=f"sd{z}")
        t_hi = E("dve", lambda e, z=z: e._custom_dve(
            Q_RNE, out=HIQE[:], in0=WST[z][:], s0=C_E4), waits=[t_d] + fw)
        t_q = E("dve", lambda e, z=z, qq=qq, u=u: e._custom_dve(
            Q_SUBN, out=W2Q[:, qq * 8192 + u * 2048: qq * 8192 + (u + 1) * 2048],
            in0=HIQE[:], in1=WST[z][:], s0=A_E4, imm2=THR_E4), waits=fw + [t_d])
        wst_free[z] = t_q
        return t_q

    ps_free = [None] * 8
    class _PFa:
        def __getitem__(self, i): return ps_free[i]
        def __setitem__(self, i, v): ps_free[i] = v
    class _PFb:
        def __getitem__(self, i): return ps_free[4 + i]
        def __setitem__(self, i, v): ps_free[4 + i] = v
    psa_free = _PFa()
    psb_free = _PFb()

    def ln_stats_tile(j, src_tok, sq_tok, src_sb):
        seen = set()
        for sc in range(NSC):
            w_all = [src_tok, t_consts, psa_free[sc]]
            w = [t for t in w_all if t is not None and t not in seen]
            seen.update(w)
            E("pe", lambda e, j=j, sc=sc, src_sb=src_sb: e.matmul(
                PSA[sc][:, :], ONES[:],
                src_sb[:, j * 2048 + sc * 512: j * 2048 + (sc + 1) * 512],
                start=(j == 0), stop=(j == NJ - 1)), waits=w)
        last_pe = None
        for sc in range(NSC):
            w_all = [sq_tok, t_consts, psb_free[sc]]
            w = [t for t in w_all if t is not None and t not in seen]
            seen.update(w)
            last_pe = E("pe", lambda e, j=j, sc=sc: e.matmul(
                PSB[sc][:, :], ONES[:], XSQ[j % 2][:, sc * 512:(sc + 1) * 512],
                start=(j == 0), stop=(j == NJ - 1)), waits=w)
        return last_pe

    def ln_stats_evict(last_pe, extra_a, extra_b):
        ev = []
        for sc in range(NSC):
            ta = E("act", lambda e, sc=sc: e.activation(
                RA[:, sc * 512:(sc + 1) * 512], PSA[sc][:, :], AF.Copy),
                waits=[last_pe] + extra_a)
            tb = E("act", lambda e, sc=sc: e.activation(
                RB[:, sc * 512:(sc + 1) * 512], PSB[sc][:, :], AF.Copy),
                waits=[last_pe] + extra_b)
            psa_free[sc] = ta
            psb_free[sc] = tb
            ev.extend([ta, tb])
        return ev

    def ln_rows(ev, rbc_free, rmbc_free):
        t1 = E("dve", lambda e: e.tensor_scalar_mul(RA[:], RA[:], INV_D), waits=ev)
        t3 = E("dve", lambda e: e.tensor_mul(RC[:], RA[:], RA[:]), waits=ev)
        t4 = E("dve", lambda e: e.scalar_tensor_tensor(RB[:], RB[:], INV_D, RC[:],
                                                       AOP.mult, AOP.subtract), waits=ev)
        t6 = E("act", lambda e: e.activation(RC[:], RB[:], AF.Ln,
                                             bias=EPSC[:, 0:1], scale=1.0),
               waits=[t4, t_consts])
        t9 = E("act", lambda e: e.activation(RBC[:], RC[:], AF.Exp, scale=-0.5),
               waits=[rbc_free])
        t8 = E("dve", lambda e: e.tensor_mul(RA[:], RA[:], RBC[:]), waits=[t9])
        t10 = E("dve", lambda e: e.tensor_copy(RMBC[:], RA[:]),
                waits=[rmbc_free])
        return t9, t10

    t_xcast = [None] * NJ
    t_xsq = [None] * NJ
    pe_sq1 = [None] * NJ
    for j in range(NJ):
        if j % 2 == 0:
            k = j // 2
            tc = E("act", lambda e, k=k: e.activation(
                XT[:, k * 4096:(k + 1) * 4096], XSTP[[0, 1, 2, 0][k]][:],
                AF.Copy), waits=[t_xdma[j], t_xdma[j + 1]])
            t_xcast[j] = t_xcast[j + 1] = tc
        t_xsq[j] = E("dve", lambda e, j=j: e.tensor_mul(
            XSQ[j % 2][:], XT[:, j * 2048:(j + 1) * 2048],
            XT[:, j * 2048:(j + 1) * 2048]),
            waits=[t_xcast[j]] + ([pe_sq1[j - 2]] if j >= 2 else []))
        assert t_xsq[j] == ("dve", _sq_pos[j]), (t_xsq[j], j)
        pe_sq1[j] = ln_stats_tile(j, t_xcast[j], t_xsq[j], XT)
    stream_w("wv", 0, wvB, None, units=(0, 1))
    r1c, rm1c = [None] * NSC, [None] * NSC
    t_xn = [None] * NJ
    t_xn_c = {}
    ev1ab = []
    for sc in range(NSC):
        s0, s1 = sc * 512, (sc + 1) * 512
        ta = E("act", lambda e, sc=sc, s0=s0, s1=s1: e.activation(
            RA[:, s0:s1], PSA[sc][:, :], AF.Copy), waits=[pe_sq1[-1]])
        tb = E("act", lambda e, sc=sc, s0=s0, s1=s1: e.activation(
            RB[:, s0:s1], PSB[sc][:, :], AF.Copy), waits=[pe_sq1[-1]])
        psa_free[sc] = ta
        psb_free[sc] = tb
        ev1ab.append((ta, tb))
    for sc in range(NSC):
        s0, s1 = sc * 512, (sc + 1) * 512
        ta, tb = ev1ab[sc]
        t1 = E("dve", lambda e, s0=s0, s1=s1: e.tensor_scalar_mul(
            RA[:, s0:s1], RA[:, s0:s1], INV_D), waits=[ta])
        t3 = E("dve", lambda e, s0=s0, s1=s1: e.tensor_mul(
            RC[:, s0:s1], RA[:, s0:s1], RA[:, s0:s1]), waits=[ta])
        t4 = E("dve", lambda e, s0=s0, s1=s1: e.scalar_tensor_tensor(
            RB[:, s0:s1], RB[:, s0:s1], INV_D, RC[:, s0:s1],
            AOP.mult, AOP.subtract), waits=[ta, tb])
        t6 = E("act", lambda e, s0=s0, s1=s1: e.activation(
            RC[:, s0:s1], RB[:, s0:s1], AF.Ln, bias=EPSC[:, 0:1], scale=1.0),
            waits=[t4, t_consts])
        r1c[sc] = E("act", lambda e, s0=s0, s1=s1: e.activation(
            RBC[:, s0:s1], RC[:, s0:s1], AF.Exp, scale=-0.5))
        if sc == 0:
            stream_w("wv", 0, wvB, None, units=(2, 3))
        elif sc < 3:
            stream_w("wq", 1, wqB, None, units=(2 * (sc - 1), 2 * sc - 1))
        xn3 = XNT[:].rearrange("p (a t) -> p a t", a=NJ)[:, :, s0:s1]
        xt3 = XT[:].rearrange("p (a t) -> p a t", a=NJ)[:, :, s0:s1]
        E("dve", lambda e, xn3=xn3, xt3=xt3, s0=s0, s1=s1: e.tensor_tensor(
            xn3, xt3, RBC[:, s0:s1].rearrange("p (a t) -> p a t", a=1)
            .broadcast_to((128, NJ, s1 - s0)), AOP.mult),
            waits=[r1c[sc]] + [t_xcast[2 * k] for k in range(4)])
        t8 = E("dve", lambda e, s0=s0, s1=s1: e.tensor_mul(
            RA[:, s0:s1], RA[:, s0:s1], RBC[:, s0:s1]), waits=[r1c[sc]])
        rm1c[sc] = E("dve", lambda e, s0=s0, s1=s1: e.tensor_copy(
            RMBC[:, s0:s1], RA[:, s0:s1]))
        tsub = E("dve", lambda e, xn3=xn3, s0=s0, s1=s1: e.tensor_tensor(
            xn3, xn3, RMBC[:, s0:s1].rearrange("p (a t) -> p a t", a=1)
            .broadcast_to((128, NJ, s1 - s0)), AOP.subtract), waits=[rm1c[sc]])
        for j in range(NJ):
            t_xn_c[(j, sc)] = tsub
    r1, rm1 = r1c[-1], rm1c[-1]
    for j in range(NJ):
        t_xn[j] = t_xn_c[(j, NSC - 1)]
    t_xn_all = t_xn[-1]


    bank = [0]

    def proj(unit_toks, slot, je, sc, rhs_sb, rhs_tok):
        z = bank[0] % 8
        bank[0] += 1
        tok = None
        pf = ps_free[z]
        seen = set()
        for jd in range(NJ):
            w_all = [unit_toks[jd // 2],
                     rhs_tok[jd] if isinstance(rhs_tok, list) else rhs_tok, pf]
            w = [t for t in w_all if t is not None and t not in seen]
            seen.update(w)
            tok = E("pe", lambda e, slot=slot, jd=jd, je=je, sc=sc, z=z: e.matmul(
                PS[z][:, :],
                WBF[slot][:, (jd * 8 + je) * 128:(jd * 8 + je + 1) * 128],
                rhs_sb[:, jd * 2048 + sc * 512: jd * 2048 + (sc + 1) * 512],
                start=(jd == 0), stop=(jd == NJ - 1)), waits=w)
        return z, tok

    w2_units = [(qq, u) for qq in range(4) for u in range(4)]
    r2k, rm2k = [None], [None]
    w2_k = [0]
    t_w2l = [None]

    def w2_unit():
        k = w2_k[0]
        w2_k[0] += 1
        qq, u = w2_units[k]
        if k == 0:
            fw = [r1]
        elif k in (8, 12):
            fw = [rm2k[0], r2k[0]]
        else:
            fw = []
        t_w2l[0] = stream_w2_unit(qq, u, fw)

    last_v_mm = [None]
    last_q_mm = [None]
    t_scan = {}
    vq_it = [0]
    for sc in range(NSC):
        for je in range(NJ):
            zv = je % 2
            z, tmm = proj(t_wbf["wv"], 0, je, sc, XNT,
                          [t_xn_c[(jd, sc)] for jd in range(NJ)])
            last_v_mm[0] = tmm
            w = [tmm]
            if je >= 2:
                w.append(t_scan[(je - 2, sc)])
            tev = E("act", lambda e, z=z, zv=zv, sc=sc: e.activation(
                VST[zv][:, sc * 512:(sc + 1) * 512], PS[z][:, :], AF.Copy), waits=w)
            ps_free[z] = tev
            if sc == 0:
                init = 0.0
            else:
                init = ST[:, je * 2048 + sc * 512 - 1: je * 2048 + sc * 512]
            t_scan[(je, sc)] = E("dve", lambda e, je=je, zv=zv, sc=sc, init=init:
                                 e.tensor_tensor_scan(
                ST[:, je * 2048 + sc * 512: je * 2048 + (sc + 1) * 512],
                LAM[:, je:je + 1].broadcast_to((128, 512)),
                VST[zv][:, sc * 512:(sc + 1) * 512], init, AOP.mult, AOP.add),
                waits=[tev, t_consts, t_scan.get((je, sc - 1))])
            vq_it[0] += 1

    t_y2 = {}
    for je in range(NJ):
        for sc in range(NSC):
            z, tmm = proj(t_wbf["wq"], 1, je, sc, XNT, t_xn)
            tev = E("dve", lambda e, z=z, je=je, sc=sc: e.tensor_tensor(
                ST[:, je * 2048 + sc * 512: je * 2048 + (sc + 1) * 512],
                PS[z][:, :],
                ST[:, je * 2048 + sc * 512: je * 2048 + (sc + 1) * 512], AOP.mult),
                waits=[tmm, t_scan[(je, sc)]])
            ps_free[z] = tev
            t_y2[(je, sc)] = tev
            last_q_mm[0] = tmm
            if vq_it[0] in (34, 39, 44, 49, 54, 59):
                w2_unit()
            vq_it[0] += 1
    t_y2_all = t_y2[(NJ - 1, NSC - 1)]

    stream_w("wg", 0, wgB, last_v_mm[0], cast_eng="dve")

    t_sq2 = [None] * NJ
    pe_sq2 = [None] * NJ
    for j in range(NJ):
        t_sq2[j] = E("act", lambda e, j=j: e.activation(
            XSQ[j % 2][:], ST[:, j * 2048:(j + 1) * 2048], AF.Square),
            waits=[t_y2[(j, NSC - 1)], t_xn_all, rm1]
            + ([pe_sq2[j - 2]] if j >= 2 else [pe_sq1[-1]]))
        pe_sq2[j] = ln_stats_tile(j, t_y2[(j, NSC - 1)], t_sq2[j], ST)
    ev2 = ln_stats_evict(pe_sq2[-1], [t_xn_all], [])
    r2, rm2 = ln_rows(ev2, t_xn_all, t_xn_all)
    r2k[0], rm2k[0] = r2, rm2
    t_gate = {}
    t_gate_col = [None] * NSC
    last_g_mm = [None]
    slot_last = [None] * 8
    g8 = 0
    for je in range(NJ):
        o0, o1 = je * 2048, (je + 1) * 2048
        E("dve", lambda e, o0=o0, o1=o1: e.tensor_tensor(
            ST[:, o0:o1], ST[:, o0:o1], RBC[:], AOP.mult),
            waits=[r2, pe_sq2[NJ - 1]])
        t_apl = E("dve", lambda e, o0=o0, o1=o1, je=je: e._custom_dve(
            Q_APL, out=ST[:, o0:o1], in0=ST[:, o0:o1], in1=RMBC[:],
            s0=GAM[:, je:je + 1], s1=BET[:, je:je + 1]),
            waits=[rm2, t_consts])
        for sc in range(NSC):
            zv, col = g8 % 2, (g8 // 2) % 4
            z, tmm = proj(t_wbf["wg"], 0, je, sc, XNT, t_xn)
            w = [tmm]
            if slot_last[g8 % 8] is not None:
                w.append(slot_last[g8 % 8])
            else:
                w.append(t_scan[(6 + zv, col)])
            tsil = E("act", lambda e, z=z, zv=zv, col=col: e.activation(
                VST[zv][:, col * 512:(col + 1) * 512], PS[z][:, :], AF.Silu),
                waits=w)
            ps_free[z] = tsil
            last_g_mm[0] = tmm
            s0_, s1_ = je * 2048 + sc * 512, je * 2048 + (sc + 1) * 512
            t_gate[(je, sc)] = E("dve", lambda e, zv=zv, col=col, s0_=s0_, s1_=s1_:
                                 e.tensor_tensor(
                ST[:, s0_:s1_], ST[:, s0_:s1_],
                VST[zv][:, col * 512:(col + 1) * 512], AOP.mult), waits=[tsil])
            slot_last[g8 % 8] = t_gate[(je, sc)]
            g8 += 1
        if je % 2 == 1:
            stream_w("wo", 1, woB, last_q_mm[0], units=(je // 2,))
    t_gate_all = t_gate[(NJ - 1, NSC - 1)]

    t_attn = {}
    last_a_mm = [None]
    it = 0
    for sc in range(NSC):
        for ke in range(NJ):
            z = bank[0] % 8
            bank[0] += 1
            tok = None
            pf = ps_free[z]
            seen = set()
            for jd in range(NJ):
                w_all = [t_wbf["wo"][jd // 2], t_gate[(jd, sc)], pf]
                w = [t for t in w_all if t is not None and t not in seen]
                seen.update(w)
                tok = E("pe", lambda e, jd=jd, ke=ke, sc=sc, z=z: e.matmul(
                    PS[z][:, :],
                    WBF[1][:, (jd * 8 + ke) * 128:(jd * 8 + ke + 1) * 128],
                    ST[:, jd * 2048 + sc * 512: jd * 2048 + (sc + 1) * 512],
                    start=(jd == 0), stop=(jd == NJ - 1)), waits=w)
            tev = E("dve", lambda e, z=z, ke=ke, sc=sc: e.scalar_tensor_tensor(
                XT[:, ke * 2048 + sc * 512: ke * 2048 + (sc + 1) * 512],
                PS[z][:, :], 1.0,
                XT[:, ke * 2048 + sc * 512: ke * 2048 + (sc + 1) * 512],
                AOP.mult, AOP.add), waits=[tok, t_xcast[ke]])
            ps_free[z] = tev
            t_attn[(ke, sc)] = tev
            last_a_mm[0] = tok
            if it in (0, 6, 12, 18):
                w2_unit()
            it += 1
    t_attn_all = t_attn[(NJ - 1, NSC - 1)]

    t_out_dma = []
    outs_free = {}
    t_hq_prev = None
    slot_prev_use = {0: last_g_mm[0], 1: last_a_mm[0]}
    for sh in range(NSH):
        t_hq = {}
        for qq in range(4):
            slot = qq % 2
            if f"w1_{sh}_{qq}" not in t_wbf:
                stream_w(f"w1_{sh}_{qq}", slot, w1B[qq], slot_prev_use[slot])
            for jhl in range(8):
                jh = qq * 8 + jhl
                if sh == 0 and (qq * 8 + jhl) in (2, 7, 12, 17, 22, 27):
                    w2_unit()
                for scl in range(2):
                    sc = sh * 2 + scl
                    z = bank[0] % 8
                    bank[0] += 1
                    tok = None
                    pf = ps_free[z]
                    seen = set()
                    for jd in range(NJ):
                        w_all = [t_wbf[f"w1_{sh}_{qq}"][jhl // 2], t_xn[jd], pf]
                        w = [t for t in w_all if t is not None and t not in seen]
                        seen.update(w)
                        tok = E("pe", lambda e, jd=jd, jhl=jhl, sc=sc, z=z, slot=slot: e.matmul(
                            PS[z][:, :],
                            WBF[slot][:, (jhl * 8 + jd) * 128:(jhl * 8 + jd + 1) * 128],
                            XNT[:, jd * 2048 + sc * 512: jd * 2048 + (sc + 1) * 512],
                            start=(jd == 0), stop=(jd == NJ - 1)), waits=w)
                    w2_ = [tok, t_consts]
                    if sh == 1:
                        w2_.append(t_hq_prev)
                    thq = E("dve", lambda e, z=z, jh=jh, scl=scl: e._custom_dve(
                        Q_HQ, out=HQ[:, jh * 1024 + scl * 512: jh * 1024 + (scl + 1) * 512],
                        in0=PS[z][:, :], s0=B1C[:, jh:jh + 1], s1=C_E4), waits=w2_)
                    ps_free[z] = thq
                    t_hq[(jh, scl)] = thq
            slot_prev_use[slot] = tok
        t_hq_all = t_hq[(NH - 1, 1)]
        if sh + 1 < NSH:
            for qq in range(2):
                stream_w(f"w1_{sh + 1}_{qq}", qq, w1B[qq], slot_prev_use[qq])
        pe_hq_last = None
        for scl in range(2):
            sc = sh * 2 + scl
            for k in range(NJ):
                z = bank[0] % 8
                bank[0] += 1
                tok = None
                pf = ps_free[z]
                seen = set()
                for jp in range(NH // 2):
                    w_all = [t_hq_all, t_w2l[0], t_b2q, pf, t_attn_all]
                    w = [t for t in w_all if t is not None and t not in seen]
                    seen.update(w)
                    tok = E("pe", lambda e, jp=jp, k=k, scl=scl, z=z: e.matmul(
                        PS[z][:, :],
                        W2Q[:, (jp * 8 + k) * 256:(jp * 8 + k) * 256 + 256]
                        .rearrange("p (i c) -> p i c", i=2),
                        HQ[:].rearrange("p (jh s) -> p jh s", jh=32)
                        [:, 2 * jp:2 * jp + 2, scl * 512:(scl + 1) * 512],
                        start=(jp == 0), stop=(jp == NH // 2 - 1),
                        perf_mode=mybir.MatmulPerfMode.DoubleRow), waits=w)
                pe_hq_last = tok
                oslc = (sc % 2) * 1024 + (k % 2) * 512
                tq = E("dve", lambda e, z=z, k=k, sc=sc, oslc=oslc: e._custom_dve(
                    Q_OUT, out=OUTS[:, oslc:oslc + 512],
                    in0=PS[z][:, :],
                    in1=XT[:, k * 2048 + sc * 512: k * 2048 + (sc + 1) * 512],
                    s0=B2Q[:, k:k + 1], s1=C_F16),
                    waits=[tok, t_attn[(k, sc)], outs_free.get(oslc)])
                ps_free[z] = tq
                td = E("act", lambda e, k=k, sc=sc, oslc=oslc: e.dma_start(
                    outT[k * 128:(k + 1) * 128, sc * 512:(sc + 1) * 512],
                    OUTS[:, oslc:oslc + 512]), waits=[tq], sem=f"gdo{oslc // 512}")
                t_out_dma.append(td)
                outs_free[oslc] = td
        t_hq_prev = pe_hq_last

    final_waits = [(s, p.n[s]) for s in ("gdo0", "gdo1", "gdo2", "gdo3")]

    @blk.tensor
    def _(e):
        for go in p.q["pe"]:
            go(e, sems)

    @blk.vector
    def _(e):
        for go in p.q["dve"]:
            go(e, sems)

    @blk.scalar
    def _(e):
        for go in p.q["act"]:
            go(e, sems)

    @blk.sync
    def _(e):
        for go in p.q["sd"]:
            go(e, sems)

    @blk.gpsimd
    def _(e):
        for go in p.q["gd"]:
            go(e, sems)

    p.q["dve"].append(lambda e, sems: [e.wait_ge(sems[s], v) for s, v in final_waits] and None)

    st.close()
    return nc


def _prep_inputs(L0_attn_x, L0_attn_decay, L0_attn_Wq, L0_attn_Wv, L0_attn_Wo,
                 L0_attn_Wg, L0_attn_ln_gamma, L0_attn_ln_beta,
                 L0_ffn_w1, L0_ffn_b1, L0_ffn_w2, L0_ffn_b2):
    f = np.float32

    def blk1024(W):
        return np.ascontiguousarray(
            np.asarray(W, f).T.reshape(8, 128, 8, 128).transpose(1, 0, 2, 3).reshape(128, 8192))

    w1B = np.ascontiguousarray(
        np.asarray(L0_ffn_w1, f).T.reshape(8, 128, 32, 128)
        .transpose(2, 1, 0, 3)
        .reshape(4, 8, 128, 8, 128)
        .transpose(0, 2, 1, 3, 4)
        .reshape(4, 128, 8192))
    w2B = np.ascontiguousarray(
        np.asarray(L0_ffn_w2, f).T.reshape(16, 2, 128, 8, 128)
        .transpose(2, 0, 3, 1, 4)
        .reshape(128, 4, 8192)
        .transpose(1, 0, 2))
    w2B = np.ascontiguousarray(w2B)
    shared = dict(
        wvB=blk1024(L0_attn_Wv), wqB=blk1024(L0_attn_Wq),
        wgB=blk1024(L0_attn_Wg), woB=blk1024(L0_attn_Wo),
        w1B=w1B, w2B=w2B,
        b1c=np.ascontiguousarray(np.asarray(L0_ffn_b1, f).reshape(32, 128).T),
        b2c=np.ascontiguousarray(np.asarray(L0_ffn_b2, f).reshape(8, 128).T),
        gamc=np.ascontiguousarray(np.asarray(L0_attn_ln_gamma, f).reshape(8, 128).T),
        betc=np.ascontiguousarray(np.asarray(L0_attn_ln_beta, f).reshape(8, 128).T),
        lamc=np.ascontiguousarray(np.repeat(np.asarray(L0_attn_decay, f), 64).reshape(8, 128).T),
        onesb=np.ones((128, 128), dtype=ml_dtypes.bfloat16),
        epsc=np.full((128, 1), 1e-5, dtype=f),
    )
    in_maps = []
    for c in range(8):
        m = dict(shared)
        m["xT"] = np.ascontiguousarray(np.asarray(L0_attn_x[c], f).T)
        in_maps.append(m)
    return in_maps


_NC_CACHE = {}


def kernel(**inputs):
    in_maps = _prep_inputs(**inputs)
    if "nc" not in _NC_CACHE:
        nc = bass.Bass("TRN2", target_bir_lowering=False)
        build(nc)
        mybir.codegen_inst_isa_subclasses(nc)
        _NC_CACHE["nc"] = nc
    nc = _NC_CACHE["nc"]
    res = run_bass_kernel_spmd(nc, in_maps, core_ids=list(range(8)))
    out = np.stack([np.asarray(r["outT"], dtype=np.float32).T for r in res.results])
    return np.ascontiguousarray(out)
